# version 04
# baseline (speedup 1.0000x reference)
import math
import numpy as np
from contextlib import ExitStack

import concourse.bass as bass
import concourse.mybir as mybir
from concourse.bass_utils import run_bass_kernel_spmd

F32 = mybir.dt.float32
BF16 = mybir.dt.bfloat16
FP8 = mybir.dt.float8e4
U8 = mybir.dt.uint8
AF = mybir.ActivationFunctionType
OP = mybir.AluOpType
AX = mybir.AxisListType

D = 2048
NCH = 16
TPC = 2048
NB = 4
QB = 512
EXT = [8, 16, 24, 32]
A_SLOTS = [0, 2, 5, 7]
B_SLOTS = [1, 3, 4, 6]
NH = 16
DFF0 = 6144
NI0 = 48
NEB = 32
NS = 4
MASKNEG = -1.0e30
SCALE = 1.0 / math.sqrt(128.0)
PHASES = 6

ENGINES = ("sync", "gpsimd", "scalar", "vector", "tensor")


class Tok:
    __slots__ = ("sem", "val")

    def __init__(self, sem, val):
        self.sem = sem
        self.val = val


class Plan:
    def __init__(self):
        self.q = {e: [] for e in ENGINES}
        self.cnt = {s: 0 for s in ("sq", "sg", "pe", "act", "dve", "gp", "cc")}
        self.track = {}
        self.pending_fence = {}

    def _bw(self, name):
        e = self.track.get(name)
        if not e:
            return []
        out = list(e["r"])
        if e["w"] is not None:
            out.append(e["w"])
        return out

    def _br(self, name):
        e = self.track.get(name)
        return [e["w"]] if e and e["w"] is not None else []

    def _setw(self, name, tok):
        self.track[name] = {"w": tok, "r": []}

    def _addr(self, name, tok):
        e = self.track.setdefault(name, {"w": None, "r": []})
        e["r"].append(tok)

    def barrier(self):
        return [Tok(s, v) for s, v in self.cnt.items() if v > 0]

    def fence(self):
        toks = self.barrier()
        for e in ENGINES:
            self.pending_fence[e] = toks

    def _push(self, eng, waits, fn, sem, incby):
        ws = list(t for t in waits if t is not None)
        if eng in self.pending_fence:
            ws = list(self.pending_fence.pop(eng)) + ws
        ws = tuple(ws)
        if sem is not None:
            self.cnt[sem] += incby
            tok = Tok(sem, self.cnt[sem])
        else:
            tok = None
        self.q[eng].append((ws, fn, sem, incby))
        return tok

    def dma(self, qeng, out, in_, w=(), r=(), extra=()):
        sem = "sq" if qeng == "sync" else "sg"
        waits = list(extra)
        for b in w:
            waits += self._bw(b)
        for b in r:
            waits += self._br(b)

        def fn(eng):
            return eng.dma_start(out=out, in_=in_)

        tok = self._push(qeng, waits, fn, sem, 16)
        for b in w:
            self._setw(b, tok)
        for b in r:
            self._addr(b, tok)
        return tok

    def op(self, ename, fname, *args, w=(), r=(), extra=(), grp=False, **kw):
        sem = {"tensor": "pe", "scalar": "act", "vector": "dve",
               "gpsimd": "gp"}[ename]
        waits = list(extra)
        for b in w:
            if not grp:
                waits += self._bw(b)
        for b in r:
            waits += self._br(b)

        def fn(eng):
            return getattr(eng, fname)(*args, **kw)

        tok = self._push(ename, waits, fn, sem, 1)
        for b in w:
            self._setw(b, tok)
        for b in r:
            self._addr(b, tok)
        return tok

    def collective(self, ins, outs, groups, extra=()):
        def fn(eng):
            return eng.collective_compute(
                "AllGather", OP.bypass, replica_groups=groups,
                ins=ins, outs=outs)

        return self._push("gpsimd", list(extra), fn, "cc", 1)

    def _dma_boundaries(self):
        bounds = {}
        for qeng, sem in (("sync", "sq"), ("gpsimd", "sg")):
            total = 0
            bs = [0]
            for ws, fn, s, incby in self.q[qeng]:
                if ws:
                    if bs[-1] != total:
                        bs.append(total)
                if s == sem:
                    total += incby
            if bs[-1] != total:
                bs.append(total)
            bounds[sem] = bs
        return bounds

    def emitter(self, sems):
        import bisect
        bounds = self._dma_boundaries()

        def rnd(sem, val):
            bs = bounds.get(sem)
            if bs is None:
                return val
            i = bisect.bisect_left(bs, val)
            return bs[i] if i < len(bs) else bs[-1]

        def mk(ename):
            own = {"sync": "sq", "gpsimd": "sg"}.get(ename)

            def body(eng):
                lastwait = {}
                total = 0
                for ws, fn, sem, incby in self.q[ename]:
                    if own is not None and ws and total > lastwait.get(own, -1):
                        eng.wait_ge(sems[own], total)
                        lastwait[own] = total
                    for t in ws:
                        v = rnd(t.sem, t.val)
                        if lastwait.get(t.sem, -1) >= v:
                            continue
                        eng.wait_ge(sems[t.sem], v)
                        lastwait[t.sem] = v
                    ins = fn(eng)
                    if sem is not None:
                        ins.then_inc(sems[sem], incby)
                        if sem == own:
                            total += incby
            return body
        return mk


def build_program():
    nc = bass.Bass("TRN2", target_bir_lowering=False)
    din = {}

    def dt_in(name, shape):
        din[name] = nc.dram_tensor(name, list(shape), F32, kind="ExternalInput")
        return din[name]

    x0 = dt_in("x0_fm", (NCH * 128, TPC))
    masks = nc.dram_tensor("masks", [NB, 8, 128, QB], BF16, kind="ExternalInput")
    L0WK = dt_in("L0_attn_W_K", (NH, 128, D))
    L0WQ = dt_in("L0_attn_W_Q", (NH, 128, D))
    L0WV = dt_in("L0_attn_W_V", (NH, 128, D))
    L0WO = dt_in("L0_attn_W_O", (D, D))
    L0GW = dt_in("L0_ffn_gate_w", (D, DFF0))
    L0UW = dt_in("L0_ffn_up_w", (D, DFF0))
    L0DW = dt_in("L0_ffn_down_w", (DFF0, D))
    L0GS = dt_in("L0_ffn_gate_scale", (DFF0,))
    L0US = dt_in("L0_ffn_up_scale", (DFF0,))
    L0DS = dt_in("L0_ffn_down_scale", (D,))
    L1WQKV = dt_in("L1_attn_Wqkv", (3 * D, D))
    L1BQKV = dt_in("L1_attn_bqkv", (3 * D,))
    L1WOUT = dt_in("L1_attn_Wout", (D, D))
    L1BOUT = dt_in("L1_attn_bout", (D,))
    L1GW = dt_in("L1_ffn_gate_w", (8, D))
    L1WD = dt_in("L1_ffn_wd", (8, 512, D))
    L1BD = dt_in("L1_ffn_bd", (8, 512))
    L1WU = dt_in("L1_ffn_wu", (8, D, 512))
    L1BU = dt_in("L1_ffn_bu", (8, D))
    L1WSD = dt_in("L1_ffn_ws_d", (512, D))
    L1BSD = dt_in("L1_ffn_bs_d", (512,))
    L1WSU = dt_in("L1_ffn_ws_u", (D, 512))
    L1BSU = dt_in("L1_ffn_bs_u", (D,))

    yout = nc.dram_tensor("y_fm", [NCH * 128, TPC], F32, kind="ExternalOutput")

    xsd = nc.dram_tensor("x_scr", [NCH, 128, TPC], F32)
    gu8s = nc.dram_tensor("gu8s", [NI0, 128, 2 * D], U8)
    d8s = nc.dram_tensor("d8s", [NI0, 128, D], U8)
    qshd = nc.dram_tensor("qsh", [NH, 128, TPC], BF16)
    qsld = nc.dram_tensor("qsl", [NH, 128, TPC], BF16)
    kvown = nc.dram_tensor("kv_own", [NB, 4, NH, 128, QB], BF16)
    kvg = nc.dram_tensor("kv_g", [2 * NB, 4, NH, 128, QB], BF16)
    aohd = nc.dram_tensor("aoh", [NH, 128, TPC], BF16)
    aold = nc.dram_tensor("aol", [NH, 128, TPC], BF16)
    wqTh = nc.dram_tensor("wqTh", [NI0, 128, D], BF16)
    wqTl = nc.dram_tensor("wqTl", [NI0, 128, D], BF16)
    woTh = nc.dram_tensor("woTh", [NCH, 128, D], BF16)
    woTl = nc.dram_tensor("woTl", [NCH, 128, D], BF16)
    wdTh = nc.dram_tensor("wdTh", [NEB, 128, D], BF16)
    wdTl = nc.dram_tensor("wdTl", [NEB, 128, D], BF16)
    wuTh = nc.dram_tensor("wuTh", [NCH, 128, NEB * 128], BF16)
    wuTl = nc.dram_tensor("wuTl", [NCH, 128, NEB * 128], BF16)
    wsdTh = nc.dram_tensor("wsdTh", [NS, 128, D], BF16)
    wsdTl = nc.dram_tensor("wsdTl", [NS, 128, D], BF16)
    wsuTh = nc.dram_tensor("wsuTh", [NCH, 128, 512], BF16)
    wsuTl = nc.dram_tensor("wsuTl", [NCH, 128, 512], BF16)

    p = Plan()

    with ExitStack() as ctx:
        def sb(name, shape, dt):
            return ctx.enter_context(nc.sbuf_tensor(name, list(shape), dt))

        sems = {}
        for s in ("sq", "sg", "pe", "act", "dve", "gp", "cc"):
            sems[s] = ctx.enter_context(nc.semaphore(s))

        ident = sb("ident", (128, 128), F32)
        ones128 = sb("ones128", (128, 1), F32)
        ones1 = sb("ones1", (1, 128), F32)
        ones128b = sb("ones128b", (128, 1), BF16)
        consts = sb("consts", (128, 2), F32)
        nc.const_aps.register(F32, 0.0, consts[:, 0:1])
        nc.const_aps.register(F32, 1.0e-5, consts[:, 1:2])
        gs_sb = sb("gs_sb", (128, NI0), F32)
        us_sb = sb("us_sb", (128, NI0), F32)
        dsh_sb = sb("dsh_sb", (128, NCH), F32)
        bqkv_sb = sb("bqkv_sb", (128, NI0), F32)
        bout_sb = sb("bout_sb", (128, NCH), F32)
        bd_sb = sb("bd_sb", (128, NEB), F32)
        bu_sb = sb("bu_sb", (8, D), F32)
        bsd_sb = sb("bsd_sb", (128, NS), F32)
        bsuh_sb = sb("bsuh_sb", (128, NCH), F32)
        gTh_sb = sb("gTh_sb", (128, NCH * 8), BF16)
        gTl_sb = sb("gTl_sb", (128, NCH * 8), BF16)
        rows = sb("rows", (1, 6 * QB), F32)

        pb = [ctx.enter_context(nc.psum_tensor(f"pb{i}", [128, 512], F32))
              for i in range(8)]
        PB = [f"pb{i}" for i in range(8)]

        big1 = sb("big1", (128, NI0 * QB), BF16)
        big2 = sb("big2", (128, NI0 * QB), BF16)
        af1 = big1.ap().bitcast(F32)
        xlnh = sb("xlnh", (128, NCH * QB), BF16)
        xlnl = sb("xlnl", (128, NCH * QB), BF16)
        wbh = sb("wbh", (128, D), BF16)
        wbl = sb("wbl", (128, D), BF16)
        wbh2 = sb("wbh2", (128, D), BF16)
        wbl2 = sb("wbl2", (128, D), BF16)
        w8t = sb("w8t", (128, 4 * D), U8)
        tmp = [sb(f"tmp{i}", (128, 512), F32) for i in range(7)]
        TN = [f"tmp{i}" for i in range(7)]
        xbb = [sb(f"xbb{i}", (128, 4 * 512), F32) for i in range(2)]

        def tb(i):
            return tmp[i], TN[i]

        def XK(c, t):
            return f"X{c}_{t}"

        p.op("gpsimd", "memset", consts[:, 0:1], 0.0, w=["consts"])
        p.op("gpsimd", "memset", consts[:, 1:2], 1.0e-5, w=["consts"],
             r=["consts"])
        t0 = p.op("gpsimd", "memset", ident[:, :], 0.0, w=["ident"])
        p.op("gpsimd", "affine_select", out=ident[:, :], in_=ident[:, :],
             compare_op=OP.not_equal, fill=1.0, base=0,
             pattern=[[-1, 128]], channel_multiplier=1,
             w=["ident"], extra=[t0])
        p.op("gpsimd", "memset", ones128[:, :], 1.0, w=["ones128"])
        p.op("gpsimd", "memset", ones128b[:, :], 1.0, w=["ones128b"])
        p.op("gpsimd", "memset", ones1[:, :], 1.0, w=["ones1"])

        def load_cvec(dst, dname, src, ncols):
            def fn(eng):
                return eng.dma_start(
                    out=dst[:, 0:ncols],
                    in_=bass.AP(src, 0, [[1, 128], [128, ncols]]),
                    allow_slow_non_contiguous=True)
            tok = p._push("sync", p._bw(dname), fn, "sq", 16)
            p._setw(dname, tok)
            return tok

        load_cvec(gs_sb, "gs", L0GS, NI0)
        load_cvec(us_sb, "us", L0US, NI0)
        load_cvec(dsh_sb, "dsh", L0DS, NCH)
        load_cvec(bqkv_sb, "bqkv", L1BQKV, NI0)
        load_cvec(bout_sb, "bout", L1BOUT, NCH)
        load_cvec(bd_sb, "bd", L1BD, NEB)
        load_cvec(bsd_sb, "bsd", L1BSD, NS)
        load_cvec(bsuh_sb, "bsuh", L1BSU, NCH)
        p.dma("sync", bu_sb[0:8, :], L1BU[:, :], w=["bu"])
        p.op("vector", "tensor_scalar", out=dsh_sb[:, 0:NCH],
             in0=dsh_sb[:, 0:NCH], scalar1=0.5, scalar2=None, op0=OP.mult,
             w=["dsh"], r=["dsh"])
        p.op("vector", "tensor_scalar", out=bsuh_sb[:, 0:NCH],
             in0=bsuh_sb[:, 0:NCH], scalar1=0.5, scalar2=None, op0=OP.mult,
             w=["bsuh"], r=["bsuh"])

        def split(srcap, srcbuf, hiap, hibuf, pdi, loap, lobuf, width):
            pd, pdn = tb(pdi)
            p.op("scalar", "copy", out=hiap, in_=srcap,
                 w=[hibuf], r=[srcbuf])
            p.op("vector", "tensor_sub", pd[:, 0:width], srcap, hiap,
                 w=[pdn], r=[srcbuf, hibuf])
            p.op("vector", "tensor_copy", out=loap, in_=pd[:, 0:width],
                 w=[lobuf], r=[pdn])

        def tstash(ncols, dsth, dstl, dkey):
            nT = ncols // 128
            for c in range(nT):
                bank = PB[c % 2]
                p.op("tensor", "transpose", pb[c % 2][:, 0:128],
                     af1[:, c * 128:(c + 1) * 128], ident[:, :],
                     w=[bank], r=["wsl", "ident"])
                csl = slice(c * 128, (c + 1) * 128)
                p.op("scalar", "copy", out=wbh[:, csl],
                     in_=pb[c % 2][:, 0:128], w=["wbh"], r=[bank])
                pd, pdn = tb(6)
                p.op("vector", "tensor_sub", pd[:, 0:128],
                     pb[c % 2][:, 0:128], wbh[:, csl],
                     w=[pdn], r=[bank, "wbh"])
                p.op("vector", "tensor_copy", out=wbl[:, csl],
                     in_=pd[:, 0:128], w=["wbl"], r=[pdn])
                p.dma("gpsimd", dsth[:, csl], wbh[:, csl], w=[dkey], r=["wbh"])
                p.dma("gpsimd", dstl[:, csl], wbl[:, csl], w=[dkey], r=["wbl"])

        def ln_tile(t, src, srckeys, dsth, dstl):
            for c in range(NCH):
                g, ci = c // 4, c % 4
                xb, xn = xbb[g % 2], f"xbb{g % 2}"
                if ci == 0:
                    p.dma("sync", xb[:, :], src4(src, c), w=[xn],
                          r=[srckeys(c + k) for k in range(4)])
                xbc = xb[:, ci * QB:(ci + 1) * QB]
                sq, sqn = tb(c % 2)
                p.op("scalar", "square", out=sq[:, :], in_=xbc,
                     w=[sqn], r=[xn])
                p.op("tensor", "matmul", pb[6][0:1, :], ones128[:, :],
                     xbc, start=(c == 0), stop=(c == NCH - 1),
                     w=[PB[6]], r=[xn, "ones128"], grp=(c > 0))
                p.op("tensor", "matmul", pb[7][0:1, :], ones128[:, :],
                     sq[:, :], start=(c == 0), stop=(c == NCH - 1),
                     w=[PB[7]], r=[sqn, "ones128"], grp=(c > 0))
            mean = rows[0:1, 0:QB]
            ex2 = rows[0:1, QB:2 * QB]
            var = rows[0:1, 2 * QB:3 * QB]
            rstd = rows[0:1, 3 * QB:4 * QB]
            mrs = rows[0:1, 4 * QB:5 * QB]
            p.op("vector", "tensor_scalar", out=mean, in0=pb[6][0:1, :],
                 scalar1=1.0 / D, scalar2=None, op0=OP.mult,
                 w=["rows"], r=[PB[6], "rows"])
            p.op("vector", "tensor_scalar", out=ex2, in0=pb[7][0:1, :],
                 scalar1=1.0 / D, scalar2=None, op0=OP.mult,
                 w=["rows"], r=[PB[7], "rows"])
            p.op("vector", "tensor_mul", var, mean, mean, w=["rows"], r=["rows"])
            p.op("vector", "tensor_sub", var, ex2, var, w=["rows"], r=["rows"])
            p.op("scalar", "activation", out=rstd, in_=var, func=AF.Sqrt,
                 bias=1.0e-5, scale=1.0, w=["rows"], r=["rows"])
            p.op("vector", "reciprocal", out=rstd, in_=rstd,
                 w=["rows"], r=["rows"])
            p.op("vector", "tensor_mul", mrs, mean, rstd, w=["rows"], r=["rows"])
            p.op("tensor", "matmul", pb[4][:, :], ones1[:, :], rstd,
                 start=True, stop=True, w=[PB[4]], r=["rows", "ones1"])
            p.op("tensor", "matmul", pb[5][:, :], ones1[:, :], mrs,
                 start=True, stop=True, w=[PB[5]], r=["rows", "ones1"])
            for c in range(NCH):
                g, ci = c // 4, c % 4
                xb, xn = xbb[g % 2], f"xbb{g % 2}"
                if ci == 0:
                    p.dma("sync", xb[:, :], src4(src, c), w=[xn],
                          r=[srckeys(c + k) for k in range(4)])
                xbc = xb[:, ci * QB:(ci + 1) * QB]
                v32, vn = tb(2 + c % 2)
                p.op("vector", "tensor_mul", v32[:, :], xbc, pb[4][:, :],
                     w=[vn], r=[xn, PB[4]])
                p.op("vector", "tensor_sub", v32[:, :], v32[:, :], pb[5][:, :],
                     w=[vn], r=[vn, PB[5]])
                hap, hbuf = dsth(c)
                lap, lbuf = dstl(c)
                split(v32[:, :], vn, hap, hbuf, 4 + c % 2, lap, lbuf, QB)

        def src4(src, c0):
            a0 = src(c0)
            a1 = src(c0 + 3)
            st = (a1.offset - a0.offset) // 3
            ap0 = [list(x) for x in a0.ap]
            return bass.AP(a0.tensor, a0.offset,
                           [ap0[0], [st, 4], ap0[1]])

        def xsrc(sub):
            if sub == 0:
                return lambda t: (lambda c: x0[c * 128:(c + 1) * 128,
                                               t * QB:(t + 1) * QB])
            return lambda t: (lambda c: xsd[c, :, t * QB:(t + 1) * QB])

        def xdst(sub):
            if sub == PHASES - 1:
                return lambda c, t: yout[c * 128:(c + 1) * 128,
                                         t * QB:(t + 1) * QB]
            return lambda c, t: xsd[c, :, t * QB:(t + 1) * QB]

        def xkeys(sub):
            if sub == 0:
                return lambda t: (lambda c: "x0ro")
            return lambda t: (lambda c: XK(c, t))

        def l0_quant():
            p.fence()
            for (wsrc, part2, skey, nrch, ncol) in (
                    (L0GW, 0, "G8", NCH, DFF0),
                    (L0UW, 1, "G8", NCH, DFF0),
                    (L0DW, 2, "D8", NI0, D)):
                for rch in range(nrch):
                    p.dma("sync", af1[:, 0:ncol],
                          wsrc[rch * 128:(rch + 1) * 128, :], w=["wsl"])
                    for piece in range((ncol + 2047) // 2048):
                        lo = piece * 2048
                        hi = min(ncol, lo + 2048)
                        npc = hi - lo
                        p.op("scalar", "copy",
                             out=w8t.ap().bitcast(FP8)[:, 0:npc],
                             in_=af1[:, lo:hi], w=["w8t"], r=["wsl"])
                        for i in range(npc // 128):
                            ig = piece * 16 + i
                            if part2 == 2:
                                dstap = d8s[rch, :, ig * 128:(ig + 1) * 128]
                            else:
                                off = part2 * D + rch * 128
                                dstap = gu8s[ig, :, off:off + 128]
                            p.dma("gpsimd", dstap,
                                  w8t[:, i * 128:(i + 1) * 128],
                                  w=[skey], r=["w8t"])

        def l0_ffn(sub):
            p.fence()
            src = xsrc(sub)
            dst = xdst(sub)
            keys = xkeys(sub)
            for t in range(NB):
                ln_tile(t, src(t), keys(t),
                        lambda c: (xlnh[:, c * QB:(c + 1) * QB], "XH"),
                        lambda c: (xlnl[:, c * QB:(c + 1) * QB], "XL"))
                for i in range(NI0):
                    par = i % 2
                    w8off = par * 2 * D
                    gslab = wbh if par == 0 else wbl
                    gslabn = "wbh" if par == 0 else "wbl"
                    uslab = wbh2 if par == 0 else wbl2
                    uslabn = "wbh2" if par == 0 else "wbl2"
                    p.dma("sync", w8t[:, w8off:w8off + 2 * D],
                          gu8s[i, :, :], w=[f"w8t{par}"], r=["G8"])
                    p.op("scalar", "copy", out=gslab[:, :],
                         in_=w8t.ap().bitcast(FP8)[:, w8off:w8off + D],
                         w=[gslabn], r=[f"w8t{par}"])
                    p.op("scalar", "copy", out=uslab[:, :],
                         in_=w8t.ap().bitcast(FP8)[:, w8off + D:w8off + 2 * D],
                         w=[uslabn], r=[f"w8t{par}"])
                    gb, ub = 0 + par * 2, 1 + par * 2
                    for c in range(NCH):
                        cs = slice(c * QB, (c + 1) * QB)
                        p.op("tensor", "matmul", pb[gb][:, :],
                             gslab[:, c * 128:(c + 1) * 128], xlnh[:, cs],
                             start=(c == 0), stop=False, w=[PB[gb]],
                             r=[gslabn, "XH"], grp=(c > 0))
                        p.op("tensor", "matmul", pb[gb][:, :],
                             gslab[:, c * 128:(c + 1) * 128], xlnl[:, cs],
                             start=False, stop=(c == NCH - 1),
                             w=[PB[gb]], r=["XL", gslabn], grp=True)
                    for c in range(NCH):
                        cs = slice(c * QB, (c + 1) * QB)
                        p.op("tensor", "matmul", pb[ub][:, :],
                             uslab[:, c * 128:(c + 1) * 128], xlnh[:, cs],
                             start=(c == 0), stop=False, w=[PB[ub]],
                             r=[uslabn, "XH"], grp=(c > 0))
                        p.op("tensor", "matmul", pb[ub][:, :],
                             uslab[:, c * 128:(c + 1) * 128], xlnl[:, cs],
                             start=False, stop=(c == NCH - 1),
                             w=[PB[ub]], r=["XL", uslabn], grp=True)
                    ta, tan = tb(0)
                    tu, tun = tb(1)
                    tc, tcn = tb(2)
                    p.op("scalar", "activation", out=ta[:, :],
                         in_=pb[gb][:, :], func=AF.Silu, bias=0.0,
                         scale=gs_sb[:, i:i + 1], w=[tan], r=[PB[gb], "gs"])
                    p.op("vector", "tensor_scalar", out=tu[:, :],
                         in0=pb[ub][:, :], scalar1=us_sb[:, i:i + 1],
                         scalar2=None, op0=OP.mult, w=[tun],
                         r=[PB[ub], "us"])
                    p.op("vector", "tensor_mul", tc[:, :], ta[:, :], tu[:, :],
                         w=[tcn], r=[tan, tun])
                    isl = slice(i * QB, (i + 1) * QB)
                    split(tc[:, :], tcn, big1[:, isl], "FH", 3,
                          big2[:, isl], "FL", QB)
                for rnd in range(2):
                    for i in range(NI0):
                        par = i % 2
                        w8off = par * 1024
                        dslab = wbh2 if par == 0 else wbl2
                        dslabn = "wbh2" if par == 0 else "wbl2"
                        p.dma("sync", w8t[:, w8off:w8off + 1024],
                              d8s[i, :, rnd * 1024:(rnd + 1) * 1024],
                              w=[f"w8t{par}"], r=["D8"])
                        p.op("scalar", "copy", out=dslab[:, 0:1024],
                             in_=w8t.ap().bitcast(FP8)[:, w8off:w8off + 1024],
                             w=[dslabn], r=[f"w8t{par}"])
                        isl = slice(i * QB, (i + 1) * QB)
                        for o in range(8):
                            p.op("tensor", "matmul", pb[o][:, :],
                                 dslab[:, o * 128:(o + 1) * 128], big1[:, isl],
                                 start=(i == 0), stop=False, w=[PB[o]],
                                 r=[dslabn, "FH"], grp=(i > 0))
                            p.op("tensor", "matmul", pb[o][:, :],
                                 dslab[:, o * 128:(o + 1) * 128], big2[:, isl],
                                 start=False, stop=(i == NI0 - 1), w=[PB[o]],
                                 r=["FL", dslabn], grp=True)
                    for g in range(2):
                        c0 = rnd * 8 + g * 4
                        xb, xn = xbb[g], f"xbb{g}"
                        p.dma("sync", xb[:, :], src4(src(t), c0), w=[xn],
                              r=[keys(t)(c0 + k) for k in range(4)])
                    for o in range(8):
                        oc = rnd * 8 + o
                        ta, tan = tb(o % 2)
                        tc, tcn = tb(2 + o % 2)
                        p.op("scalar", "activation", out=ta[:, :],
                             in_=pb[o][:, :], func=AF.Copy, bias=0.0,
                             scale=dsh_sb[:, oc:oc + 1], w=[tan],
                             r=[PB[o], "dsh"])
                        xb, xn = xbb[o // 4], f"xbb{o // 4}"
                        p.op("vector", "tensor_add", tc[:, :], ta[:, :],
                             xb[:, (o % 4) * QB:(o % 4 + 1) * QB],
                             w=[tcn], r=[tan, xn])
                        p.dma("gpsimd", dst(oc, t), tc[:, :],
                              w=[XK(oc, t)], r=[tcn])

        def prep_qkv_weights(layer):
            if layer == 0:
                srcs = [(role * NH + h, wsrc[h, :, :])
                        for role, wsrc in ((0, L0WQ), (1, L0WK), (2, L0WV))
                        for h in range(NH)]
            else:
                srcs = [(o, L1WQKV[o * 128:(o + 1) * 128, :])
                        for o in range(NI0)]
            for o, srcap in srcs:
                p.dma("sync", af1[:, 0:D], srcap, w=["wsl"])
                tstash(D, wqTh[o, :, :], wqTl[o, :, :], "WQT")

        def prep_wo_weights(layer):
            wsrc = L0WO if layer == 0 else L1WOUT
            for dd in range(NCH):
                p.dma("sync", af1[:, 0:D], wsrc[dd * 128:(dd + 1) * 128, :],
                      w=["wsl"])
                tstash(D, woTh[dd, :, :], woTl[dd, :, :], "WOT")

        def attention(sub, layer):
            p.fence()
            src = xsrc(sub)
            dst = xdst(sub)
            keys = xkeys(sub)
            prep_qkv_weights(layer)
            prep_wo_weights(layer)
            p.fence()

            def xh_ap(c, t):
                if c < 12:
                    return big1[:, c * 2048 + t * QB: c * 2048 + t * QB + QB]
                return xlnh[:, (c - 12) * 2048 + t * QB:
                            (c - 12) * 2048 + t * QB + QB]

            def xl_ap(c, t):
                if c < 12:
                    return big2[:, c * 2048 + t * QB: c * 2048 + t * QB + QB]
                return xlnl[:, (c - 12) * 2048 + t * QB:
                            (c - 12) * 2048 + t * QB + QB]

            for t in range(NB):
                ln_tile(t, src(t), keys(t),
                        lambda c, t=t: (xh_ap(c, t), "XH"),
                        lambda c, t=t: (xl_ap(c, t), "XL"))

            for o in range(NI0):
                role, h = o // NH, o % NH
                par = o % 2
                sh = wbh if par == 0 else wbh2
                shn = "wbh" if par == 0 else "wbh2"
                sl = wbl if par == 0 else wbl2
                sln = "wbl" if par == 0 else "wbl2"
                p.dma("sync", sh[:, :], wqTh[o, :, :], w=[shn], r=["WQT"])
                p.dma("sync", sl[:, :], wqTl[o, :, :], w=[sln], r=["WQT"])
                for t in range(NB):
                    bank = PB[4 + t % 2]
                    bk = pb[4 + t % 2]
                    for c in range(NCH):
                        p.op("tensor", "matmul", bk[:, :],
                             sh[:, c * 128:(c + 1) * 128], xh_ap(c, t),
                             start=(c == 0), stop=False, w=[bank],
                             r=[shn, "XH"], grp=(c > 0))
                        p.op("tensor", "matmul", bk[:, :],
                             sl[:, c * 128:(c + 1) * 128], xh_ap(c, t),
                             start=False, stop=False, w=[bank],
                             r=[sln, "XH"], grp=True)
                        p.op("tensor", "matmul", bk[:, :],
                             sh[:, c * 128:(c + 1) * 128], xl_ap(c, t),
                             start=False, stop=(c == NCH - 1), w=[bank],
                             r=["XL", shn], grp=True)
                    ta, tan = tb(0)
                    if layer == 1:
                        p.op("vector", "tensor_scalar", out=ta[:, :],
                             in0=bk[:, :], scalar1=bqkv_sb[:, o:o + 1],
                             scalar2=None, op0=OP.add,
                             w=[tan], r=[bank, "bqkv"])
                    else:
                        p.op("vector", "tensor_copy", out=ta[:, :],
                             in_=bk[:, :], w=[tan], r=[bank])
                    if role in (0, 1):
                        hi = (qshd[h, :, t * QB:(t + 1) * QB] if role == 0
                              else kvown[t, 0, h, :, :])
                        lo = (qsld[h, :, t * QB:(t + 1) * QB] if role == 0
                              else kvown[t, 1, h, :, :])
                        hb = tmp[1].ap().bitcast(BF16)[:, 0:QB]
                        lb = tmp[1].ap().bitcast(BF16)[:, QB:2 * QB]
                        split(ta[:, :], tan, hb, "tmp1", 2, lb, "tmp1", QB)
                        p.dma("gpsimd", hi, hb, w=["QS"], r=["tmp1"])
                        p.dma("gpsimd", lo, lb, w=["QS"], r=["tmp1"])
                    else:
                        hb = tmp[1].ap().bitcast(BF16)
                        for q4 in range(4):
                            bank2 = PB[6 + q4 % 2]
                            p.op("tensor", "transpose",
                                 pb[6 + q4 % 2][:, 0:128],
                                 ta[:, q4 * 128:(q4 + 1) * 128],
                                 ident[:, :], w=[bank2], r=[tan, "ident"])
                            qs2 = slice(q4 * 128, (q4 + 1) * 128)
                            p.op("scalar", "copy", out=hb[:, qs2],
                                 in_=pb[6 + q4 % 2][:, 0:128],
                                 w=["tmp1"], r=[bank2])
                            pd, pdn = tb(2)
                            p.op("vector", "tensor_sub", pd[:, 0:128],
                                 pb[6 + q4 % 2][:, 0:128], hb[:, qs2],
                                 w=[pdn], r=[bank2, "tmp1"])
                            p.op("vector", "tensor_copy",
                                 out=hb[:, QB + q4 * 128:QB + (q4 + 1) * 128],
                                 in_=pd[:, 0:128], w=["tmp1"], r=[pdn])
                        p.dma("gpsimd", kvown[t, 2, h, :, :], hb[:, 0:QB],
                              w=["QS"], r=["tmp1"])
                        p.dma("gpsimd", kvown[t, 3, h, :, :],
                              hb[:, QB:2 * QB], w=["QS"], r=["tmp1"])

            groups = [[0, 1], [2, 3], [4, 5], [6, 7]]
            bar2 = p.barrier()
            cc_toks = []
            for j in range(NB):
                cc_toks.append(p.collective([kvown[j]],
                                            [kvg[2 * j:2 * j + 2]],
                                            groups, extra=bar2))

            p.fence()
            for j in range(NB):
                for k in range(8):
                    p.dma("sync",
                          big2[:, (j * 8 + k) * QB:(j * 8 + k + 1) * QB],
                          masks[j, k, :, :], w=["MK"])
            for h in range(NH):
                for kind in range(4):
                    srcv = kvg[:, kind, h, :, :].transpose([1, 0, 2])
                    p.dma("sync", big1[:, kind * 4096:(kind + 1) * 4096],
                          srcv, w=["KV"], extra=list(cc_toks))
                for j in range(NB):
                    E = EXT[j]
                    qhb = tmp[5].ap().bitcast(BF16)[:, 0:QB]
                    qlb = tmp[5].ap().bitcast(BF16)[:, QB:2 * QB]
                    p.dma("sync", qhb, qshd[h, :, j * QB:(j + 1) * QB],
                          w=["tmp5"], r=["QS"])
                    p.dma("sync", qlb, qsld[h, :, j * QB:(j + 1) * QB],
                          w=["tmp5"], r=["QS"])
                    SROT = [0, 1, 5]

                    def emit_qk(c):
                        sbank = PB[SROT[c % 3]]
                        sp = pb[SROT[c % 3]]
                        kh_ap = big1[:, c * 128:(c + 1) * 128]
                        kl_ap = big1[:, 4096 + c * 128:4096 + (c + 1) * 128]
                        p.op("tensor", "matmul", sp[:, :], kh_ap, qhb,
                             start=True, stop=False, w=[sbank],
                             r=["KV", "tmp5"])
                        p.op("tensor", "matmul", sp[:, :], kl_ap, qhb,
                             start=False, stop=False, w=[sbank],
                             r=["KV", "tmp5"], grp=True)
                        p.op("tensor", "matmul", sp[:, :], kh_ap, qlb,
                             start=False, stop=True, w=[sbank],
                             r=["KV", "tmp5"], grp=True)

                    def emit_evac(c):
                        sbank = PB[SROT[c % 3]]
                        sp = pb[SROT[c % 3]]
                        vh_ap = big1[:, 8192 + c * 128:8192 + (c + 1) * 128]
                        vl_ap = big1[:, 12288 + c * 128:12288 + (c + 1) * 128]
                        if c >= E - 8:
                            k = c - (E - 8)
                            msl = slice((j * 8 + k) * QB, (j * 8 + k + 1) * QB)
                            p.op("vector", "tensor_add", sp[:, :], sp[:, :],
                                 big2[:, msl], w=[sbank], r=[sbank, "MK"])
                        pa, pan = tb(c % 2)
                        pp = tmp[2 + c % 2]
                        ppn = TN[2 + c % 2]
                        phb = pp.ap().bitcast(BF16)[:, 0:QB]
                        plb = pp.ap().bitcast(BF16)[:, QB:2 * QB]
                        p.op("scalar", "activation", out=pa[:, :],
                             in_=sp[:, :], func=AF.Exp, bias=0.0, scale=SCALE,
                             w=[pan], r=[sbank])
                        p.op("scalar", "copy", out=phb, in_=pa[:, :],
                             w=[ppn], r=[pan])
                        pd, pdn = tb(4)
                        p.op("vector", "tensor_sub", pd[:, :], pa[:, :], phb,
                             w=[pdn], r=[pan, ppn])
                        p.op("vector", "tensor_copy", out=plb, in_=pd[:, :],
                             w=[ppn], r=[pdn])
                        p.op("tensor", "matmul", pb[3][0:1, :],
                             ones128b[:, :], phb, start=(c == 0), stop=False,
                             w=[PB[3]], r=[ppn, "ones128b"], grp=(c > 0))
                        p.op("tensor", "matmul", pb[3][0:1, :],
                             ones128b[:, :], plb, start=False,
                             stop=(c == E - 1), w=[PB[3]],
                             r=[ppn, "ones128b"], grp=True)
                        p.op("tensor", "matmul", pb[2][:, :], vh_ap, phb,
                             start=(c == 0), stop=False, w=[PB[2]],
                             r=["KV", ppn], grp=(c > 0))
                        p.op("tensor", "matmul", pb[2][:, :], vl_ap, phb,
                             start=False, stop=False, w=[PB[2]],
                             r=["KV", ppn], grp=True)
                        p.op("tensor", "matmul", pb[2][:, :], vh_ap, plb,
                             start=False, stop=(c == E - 1), w=[PB[2]],
                             r=["KV", ppn], grp=True)

                    for c in range(E):
                        emit_qk(c)
                        if c >= 1:
                            emit_evac(c - 1)
                    emit_evac(E - 1)
                    sr = rows[0:1, 5 * QB:6 * QB]
                    p.op("vector", "tensor_copy", out=sr, in_=pb[3][0:1, :],
                         w=["rows"], r=[PB[3], "rows"])
                    p.op("vector", "reciprocal", out=sr, in_=sr,
                         w=["rows"], r=["rows"])
                    p.op("tensor", "matmul", pb[4][:, :], ones1[:, :], sr,
                         start=True, stop=True, w=[PB[4]], r=["rows", "ones1"])
                    rb, rbn = tb(1)
                    p.op("vector", "tensor_copy", out=rb[:, :],
                         in_=pb[4][:, :], w=[rbn], r=[PB[4]])
                    ta, tan = tb(6)
                    p.op("vector", "tensor_mul", ta[:, :], pb[2][:, :],
                         rb[:, :], w=[tan], r=[PB[2], rbn])
                    ob = tmp[5].ap().bitcast(BF16)
                    p.op("scalar", "copy", out=ob[:, 0:QB], in_=ta[:, :],
                         w=["tmp5"], r=[tan])
                    pd, pdn = tb(4)
                    p.op("vector", "tensor_sub", pd[:, :], ta[:, :],
                         ob[:, 0:QB], w=[pdn], r=[tan, "tmp5"])
                    p.op("vector", "tensor_copy", out=ob[:, QB:2 * QB],
                         in_=pd[:, :], w=["tmp5"], r=[pdn])
                    p.dma("gpsimd", aohd[h, :, j * QB:(j + 1) * QB],
                          ob[:, 0:QB], w=["AO"], r=["tmp5"])
                    p.dma("gpsimd", aold[h, :, j * QB:(j + 1) * QB],
                          ob[:, QB:2 * QB], w=["AO"], r=["tmp5"])

            p.fence()
            for t in range(NB):
                for f in range(NCH):
                    p.dma("sync", xlnh[:, f * QB:(f + 1) * QB],
                          aohd[f, :, t * QB:(t + 1) * QB], w=["XH"], r=["AO"])
                    p.dma("sync", xlnl[:, f * QB:(f + 1) * QB],
                          aold[f, :, t * QB:(t + 1) * QB], w=["XL"], r=["AO"])
                for dd in range(NCH):
                    if dd % 4 == 0:
                        xbg, xgn = xbb[(dd // 4) % 2], f"xbb{(dd // 4) % 2}"
                        p.dma("sync", xbg[:, :], src4(src(t), dd), w=[xgn],
                              r=[keys(t)(dd + k) for k in range(4)])
                    par = dd % 2
                    sh = wbh if par == 0 else wbh2
                    shn = "wbh" if par == 0 else "wbh2"
                    sl = wbl if par == 0 else wbl2
                    sln = "wbl" if par == 0 else "wbl2"
                    p.dma("sync", sh[:, :], woTh[dd, :, :], w=[shn],
                          r=["WOT"])
                    p.dma("sync", sl[:, :], woTl[dd, :, :], w=[sln],
                          r=["WOT"])
                    bank = PB[4 + dd % 2]
                    bk = pb[4 + dd % 2]
                    for f in range(NCH):
                        fs = slice(f * QB, (f + 1) * QB)
                        p.op("tensor", "matmul", bk[:, :],
                             sh[:, f * 128:(f + 1) * 128], xlnh[:, fs],
                             start=(f == 0), stop=False, w=[bank],
                             r=[shn, "XH"], grp=(f > 0))
                        p.op("tensor", "matmul", bk[:, :],
                             sl[:, f * 128:(f + 1) * 128], xlnh[:, fs],
                             start=False, stop=False, w=[bank],
                             r=[sln, "XH"], grp=True)
                        p.op("tensor", "matmul", bk[:, :],
                             sh[:, f * 128:(f + 1) * 128], xlnl[:, fs],
                             start=False, stop=(f == NCH - 1), w=[bank],
                             r=["XL", shn], grp=True)
                    ta, tan = tb(0 + dd % 2)
                    if layer == 1:
                        p.op("vector", "tensor_scalar", out=ta[:, :],
                             in0=bk[:, :], scalar1=bout_sb[:, dd:dd + 1],
                             scalar2=None, op0=OP.add,
                             w=[tan], r=[bank, "bout"])
                    else:
                        p.op("vector", "tensor_copy", out=ta[:, :],
                             in_=bk[:, :], w=[tan], r=[bank])
                    xb, xn = xbb[(dd // 4) % 2], f"xbb{(dd // 4) % 2}"
                    tc, tcn = tb(2 + dd % 2)
                    p.op("vector", "tensor_add", tc[:, :], ta[:, :],
                         xb[:, (dd % 4) * QB:(dd % 4 + 1) * QB],
                         w=[tcn], r=[tan, xn])
                    p.dma("gpsimd", dst(dd, t), tc[:, :],
                          w=[XK(dd, t)], r=[tcn])

        def moe_prep():
            p.fence()
            wdflat = L1WD.reshape([8 * 512, D])
            for eb in range(NEB):
                p.dma("sync", af1[:, 0:D],
                      wdflat[eb * 128:(eb + 1) * 128, :], w=["wsl"])
                tstash(D, wdTh[eb, :, :], wdTl[eb, :, :], "WDT")
            for e in range(8):
                for dc in range(NCH):
                    p.dma("sync", af1[:, 0:512],
                          L1WU[e, dc * 128:(dc + 1) * 128, :], w=["wsl"])
                    for b4 in range(4):
                        bank = PB[b4 % 2]
                        p.op("tensor", "transpose", pb[b4 % 2][:, 0:128],
                             af1[:, b4 * 128:(b4 + 1) * 128], ident[:, :],
                             w=[bank], r=["wsl", "ident"])
                        csl = slice(b4 * 128, (b4 + 1) * 128)
                        p.op("scalar", "copy", out=wbh[:, csl],
                             in_=pb[b4 % 2][:, 0:128], w=["wbh"], r=[bank])
                        pd, pdn = tb(6)
                        p.op("vector", "tensor_sub", pd[:, 0:128],
                             pb[b4 % 2][:, 0:128], wbh[:, csl],
                             w=[pdn], r=[bank, "wbh"])
                        p.op("vector", "tensor_copy", out=wbl[:, csl],
                             in_=pd[:, 0:128], w=["wbl"], r=[pdn])
                        gsl = slice((e * 4 + b4) * 128, (e * 4 + b4 + 1) * 128)
                        p.dma("gpsimd", wuTh[dc, :, gsl], wbh[:, csl],
                              w=["WUT"], r=["wbh"])
                        p.dma("gpsimd", wuTl[dc, :, gsl], wbl[:, csl],
                              w=["WUT"], r=["wbl"])
            for s in range(NS):
                p.dma("sync", af1[:, 0:D], L1WSD[s * 128:(s + 1) * 128, :],
                      w=["wsl"])
                tstash(D, wsdTh[s, :, :], wsdTl[s, :, :], "WSDT")
            for dc in range(NCH):
                p.dma("sync", af1[:, 0:512],
                      L1WSU[dc * 128:(dc + 1) * 128, :], w=["wsl"])
                tstash(512, wsuTh[dc, :, :], wsuTl[dc, :, :], "WSUT")

        def moe_gate_prep():
            p.fence()
            p.dma("sync", af1[0:8, 0:D], L1GW[:, :], w=["wsl"])
            for c in range(NCH):
                bank = PB[c % 2]
                p.op("tensor", "transpose", pb[c % 2][0:128, 0:8],
                     af1[0:8, c * 128:(c + 1) * 128], ident[0:8, 0:8],
                     w=[bank], r=["wsl", "ident"])
                csl = slice(c * 8, (c + 1) * 8)
                p.op("scalar", "copy", out=gTh_sb[:, csl],
                     in_=pb[c % 2][0:128, 0:8], w=["gT"], r=[bank])
                pd, pdn = tb(6)
                p.op("vector", "tensor_sub", pd[:, 0:8],
                     pb[c % 2][0:128, 0:8], gTh_sb[:, csl],
                     w=[pdn], r=[bank, "gT"])
                p.op("vector", "tensor_copy", out=gTl_sb[:, csl],
                     in_=pd[:, 0:8], w=["gT2"], r=[pdn])

        def moe(sub):
            p.fence()
            src = xsrc(sub)
            dst = xdst(sub)
            keys = xkeys(sub)
            hoff, goff, woff = 0, NEB * QB, NEB * QB + NS * QB
            for t in range(NB):
                ln_tile(t, src(t), keys(t),
                        lambda c: (xlnh[:, c * QB:(c + 1) * QB], "XH"),
                        lambda c: (xlnl[:, c * QB:(c + 1) * QB], "XL"))
                for c in range(NCH):
                    cs = slice(c * QB, (c + 1) * QB)
                    gsl = slice(c * 8, (c + 1) * 8)
                    last = (c == NCH - 1)
                    p.op("tensor", "matmul", pb[0][0:8, :], gTh_sb[:, gsl],
                         xlnh[:, cs], start=(c == 0), stop=False, w=[PB[0]],
                         r=["gT", "XH"], grp=(c > 0))
                    p.op("tensor", "matmul", pb[0][0:8, :], gTl_sb[:, gsl],
                         xlnh[:, cs], start=False, stop=False, w=[PB[0]],
                         r=["gT2", "XH"], grp=True)
                    p.op("tensor", "matmul", pb[0][0:8, :], gTh_sb[:, gsl],
                         xlnl[:, cs], start=False, stop=last, w=[PB[0]],
                         r=["XL", "gT"], grp=True)
                lg, lgn = tb(0)
                p.op("vector", "tensor_copy", out=lg[0:8, :],
                     in_=pb[0][0:8, :], w=[lgn], r=[PB[0]])
                lt, ltn = tb(1)
                for st in range(4):
                    bank = PB[1 + st % 2]
                    p.op("tensor", "transpose", pb[1 + st % 2][0:128, 0:8],
                         lg[0:8, st * 128:(st + 1) * 128], ident[0:8, 0:8],
                         w=[bank], r=[lgn, "ident"])
                    p.op("vector", "tensor_copy",
                         out=lt[:, st * 8:(st + 1) * 8],
                         in_=pb[1 + st % 2][0:128, 0:8], w=[ltn], r=[bank])
                cmb, cmbn = tb(2)
                sc, scn = tb(3)
                for st in range(4):
                    lsl = slice(st * 8, (st + 1) * 8)
                    m1c, m2c = sc[:, 0:1], sc[:, 1:2]
                    w1c, w2c = sc[:, 2:3], sc[:, 3:4]
                    dfc = sc[:, 4:5]
                    msk1 = sc[:, 8:16]
                    l2b = sc[:, 16:24]
                    msk2 = sc[:, 24:32]
                    p.op("vector", "reduce_max", m1c, lt[:, lsl], axis=AX.X,
                         w=[scn], r=[ltn])
                    p.op("vector", "tensor_scalar", out=msk1, in0=lt[:, lsl],
                         scalar1=m1c, scalar2=None, op0=OP.is_equal,
                         w=[scn], r=[ltn, scn])
                    p.op("vector", "scalar_tensor_tensor", out=l2b, in0=msk1,
                         scalar=MASKNEG, in1=lt[:, lsl], op0=OP.mult,
                         op1=OP.add, w=[scn], r=[scn, ltn])
                    p.op("vector", "reduce_max", m2c, l2b, axis=AX.X,
                         w=[scn], r=[scn])
                    p.op("vector", "tensor_sub", dfc, m1c, m2c, w=[scn],
                         r=[scn])
                    p.op("scalar", "activation", out=w1c, in_=dfc,
                         func=AF.Sigmoid, bias=0.0, scale=1.0, w=[scn],
                         r=[scn])
                    p.op("vector", "tensor_scalar", out=w2c, in0=w1c,
                         scalar1=-1.0, scalar2=1.0, op0=OP.mult, op1=OP.add,
                         w=[scn], r=[scn])
                    p.op("vector", "tensor_scalar", out=msk2, in0=l2b,
                         scalar1=m2c, scalar2=None, op0=OP.is_equal,
                         w=[scn], r=[scn])
                    p.op("vector", "tensor_scalar", out=msk1, in0=msk1,
                         scalar1=w1c, scalar2=None, op0=OP.mult, w=[scn],
                         r=[scn])
                    p.op("vector", "tensor_scalar", out=msk2, in0=msk2,
                         scalar1=w2c, scalar2=None, op0=OP.mult, w=[scn],
                         r=[scn])
                    p.op("vector", "tensor_add", cmb[:, lsl], msk1, msk2,
                         w=[cmbn], r=[scn])
                cfm, cfmn = tb(0)
                for st in range(4):
                    bank = PB[1 + st % 2]
                    p.op("tensor", "transpose", pb[1 + st % 2][0:8, 0:128],
                         cmb[:, st * 8:(st + 1) * 8], ident[:, :],
                         w=[bank], r=[cmbn, "ident"])
                    p.op("vector", "tensor_copy",
                         out=cfm[0:8, st * 128:(st + 1) * 128],
                         in_=pb[1 + st % 2][0:8, 0:128], w=[cfmn], r=[bank])
                for s in range(NS):
                    p.dma("sync", wbh2[:, :], wsdTh[s, :, :], w=["wbh2"],
                          r=["WSDT"])
                    p.dma("sync", wbl2[:, :], wsdTl[s, :, :], w=["wbl2"],
                          r=["WSDT"])
                    bank = PB[4 + s % 2]
                    bk = pb[4 + s % 2]
                    for c in range(NCH):
                        cs = slice(c * QB, (c + 1) * QB)
                        last = (c == NCH - 1)
                        p.op("tensor", "matmul", bk[:, :],
                             wbh2[:, c * 128:(c + 1) * 128], xlnh[:, cs],
                             start=(c == 0), stop=False, w=[bank],
                             r=["wbh2", "XH"], grp=(c > 0))
                        p.op("tensor", "matmul", bk[:, :],
                             wbl2[:, c * 128:(c + 1) * 128], xlnh[:, cs],
                             start=False, stop=False, w=[bank],
                             r=["wbl2", "XH"], grp=True)
                        p.op("tensor", "matmul", bk[:, :],
                             wbh2[:, c * 128:(c + 1) * 128], xlnl[:, cs],
                             start=False, stop=last, w=[bank],
                             r=["XL", "wbh2"], grp=True)
                    ta, tan = tb(4)
                    p.op("scalar", "activation", out=ta[:, :], in_=bk[:, :],
                         func=AF.Gelu, bias=bsd_sb[:, s:s + 1], scale=1.0,
                         w=[tan], r=[bank, "bsd"])
                    gsl2 = slice(goff + s * QB, goff + (s + 1) * QB)
                    split(ta[:, :], tan, big1[:, gsl2], "HH", 5,
                          big2[:, gsl2], "HL", QB)
                bcb, bcbn = tb(3)
                for eb in range(NEB):
                    e = eb // 4
                    if eb % 4 == 0:
                        erow = rows[0:1, 5 * QB:6 * QB]
                        p.op("tensor", "matmul", pb[7][0:1, :],
                             ident[0:8, e:e + 1], cfm[0:8, 0:QB],
                             start=True, stop=True, w=[PB[7]],
                             r=[cfmn, "ident"])
                        p.op("vector", "tensor_copy", out=erow,
                             in_=pb[7][0:1, :], w=["rows"],
                             r=[PB[7], "rows"])
                        p.op("tensor", "matmul", pb[6][:, :], ones1[:, :],
                             erow, start=True, stop=True,
                             w=[PB[6]], r=["rows", "ones1"])
                        p.op("vector", "tensor_copy", out=bcb[:, :],
                             in_=pb[6][:, :], w=[bcbn], r=[PB[6]])
                    par = eb % 2
                    sh = wbh2 if par == 0 else wbh
                    shn = "wbh2" if par == 0 else "wbh"
                    sl = wbl2 if par == 0 else wbl
                    sln = "wbl2" if par == 0 else "wbl"
                    p.dma("sync", sh[:, :], wdTh[eb, :, :], w=[shn],
                          r=["WDT"])
                    p.dma("sync", sl[:, :], wdTl[eb, :, :], w=[sln],
                          r=["WDT"])
                    bank = PB[4 + eb % 2]
                    bk = pb[4 + eb % 2]
                    for c in range(NCH):
                        cs = slice(c * QB, (c + 1) * QB)
                        last = (c == NCH - 1)
                        p.op("tensor", "matmul", bk[:, :],
                             sh[:, c * 128:(c + 1) * 128], xlnh[:, cs],
                             start=(c == 0), stop=False, w=[bank],
                             r=[shn, "XH"], grp=(c > 0))
                        p.op("tensor", "matmul", bk[:, :],
                             sl[:, c * 128:(c + 1) * 128], xlnh[:, cs],
                             start=False, stop=False, w=[bank],
                             r=[sln, "XH"], grp=True)
                        p.op("tensor", "matmul", bk[:, :],
                             sh[:, c * 128:(c + 1) * 128], xlnl[:, cs],
                             start=False, stop=last, w=[bank],
                             r=["XL", shn], grp=True)
                    ta, tan = tb(4)
                    p.op("scalar", "activation", out=ta[:, :], in_=bk[:, :],
                         func=AF.Gelu, bias=bd_sb[:, eb:eb + 1], scale=1.0,
                         w=[tan], r=[bank, "bd"])
                    p.op("vector", "tensor_mul", ta[:, :], ta[:, :],
                         bcb[:, :], w=[tan], r=[tan, bcbn])
                    hsl = slice(hoff + eb * QB, hoff + (eb + 1) * QB)
                    split(ta[:, :], tan, big1[:, hsl], "HH", 5,
                          big2[:, hsl], "HL", QB)
                for dc in range(NCH):
                    if dc % 4 == 0:
                        xbg, xgn = xbb[(dc // 4) % 2], f"xbb{(dc // 4) % 2}"
                        p.dma("sync", xbg[:, :], src4(src(t), dc), w=[xgn],
                              r=[keys(t)(dc + k) for k in range(4)])
                    ws2a = slice(woff, woff + NEB * 128)
                    p.dma("sync", big1[:, ws2a], wuTh[dc, :, :], w=["WU"],
                          r=["WUT"])
                    p.dma("sync", big2[:, ws2a], wuTl[dc, :, :], w=["WU2"],
                          r=["WUT"])
                    p.dma("sync", wbh[:, 0:512], wsuTh[dc, :, :], w=["wbh"],
                          r=["WSUT"])
                    p.dma("sync", wbl[:, 0:512], wsuTl[dc, :, :], w=["wbl"],
                          r=["WSUT"])
                    bank = PB[4 + dc % 2]
                    bk = pb[4 + dc % 2]
                    for eb in range(NEB):
                        hsl = slice(hoff + eb * QB, hoff + (eb + 1) * QB)
                        ws2 = slice(woff + eb * 128, woff + (eb + 1) * 128)
                        p.op("tensor", "matmul", bk[:, :], big1[:, ws2],
                             big1[:, hsl], start=(eb == 0), stop=False,
                             w=[bank], r=["WU", "HH"], grp=(eb > 0))
                        p.op("tensor", "matmul", bk[:, :], big2[:, ws2],
                             big1[:, hsl], start=False, stop=False,
                             w=[bank], r=["WU2", "HH"], grp=True)
                        p.op("tensor", "matmul", bk[:, :], big1[:, ws2],
                             big2[:, hsl], start=False, stop=False,
                             w=[bank], r=["HL", "WU"], grp=True)
                    for s in range(NS):
                        gsl2 = slice(goff + s * QB, goff + (s + 1) * QB)
                        ssl = slice(s * 128, (s + 1) * 128)
                        p.op("tensor", "matmul", bk[:, :], wbh[:, ssl],
                             big1[:, gsl2], start=False, stop=False,
                             w=[bank], r=["wbh", "HH"], grp=True)
                        p.op("tensor", "matmul", bk[:, :], wbl[:, ssl],
                             big1[:, gsl2], start=False, stop=False,
                             w=[bank], r=["wbl", "HH"], grp=True)
                        p.op("tensor", "matmul", bk[:, :], wbh[:, ssl],
                             big2[:, gsl2], start=False, stop=False,
                             w=[bank], r=["HL", "wbh", "wbl"], grp=True)
                    p.op("tensor", "matmul", bk[:, :],
                         bu_sb[:, dc * 128:(dc + 1) * 128], cfm[0:8, 0:QB],
                         start=False, stop=True, w=[bank],
                         r=["bu", cfmn, "WU", "WU2", "HH", "HL"], grp=True)
                    ta, tan = tb(4 + dc % 2)
                    p.op("vector", "tensor_scalar", out=ta[:, :],
                         in0=bk[:, :], scalar1=0.5,
                         scalar2=bsuh_sb[:, dc:dc + 1],
                         op0=OP.mult, op1=OP.add,
                         w=[tan], r=[bank, "bsuh"])
                    xb, xn = xbb[(dc // 4) % 2], f"xbb{(dc // 4) % 2}"
                    tc, tcn = tb(6)
                    p.op("vector", "tensor_add", tc[:, :], ta[:, :],
                         xb[:, (dc % 4) * QB:(dc % 4 + 1) * QB],
                         w=[tcn], r=[tan, xn])
                    p.dma("gpsimd", dst(dc, t), tc[:, :],
                          w=[XK(dc, t)], r=[tcn])

        l0_quant()
        l0_ffn(0)
        if PHASES >= 2:
            attention(1, 0)
        if PHASES >= 3:
            l0_ffn(2)
        if PHASES >= 4:
            moe_prep()
            moe_gate_prep()
            moe(3)
        if PHASES >= 5:
            attention(4, 1)
        if PHASES >= 6:
            moe_gate_prep()
            moe(5)

        fin = p.barrier()
        p.q["sync"].append((tuple(fin), lambda eng: eng.nop(), None, 0))

        with nc.Block() as block:
            mk = p.emitter(sems)
            block.sync(mk("sync"))
            block.gpsimd(mk("gpsimd"))
            block.scalar(mk("scalar"))
            block.vector(mk("vector"))
            block.tensor(mk("tensor"))

    return nc



def _core_slots(cls):
    return A_SLOTS if cls == 0 else B_SLOTS


def _make_masks(cls):
    import ml_dtypes
    slots = _core_slots(cls)
    m = np.zeros((NB, 8, 128, QB), np.float32)
    qr = np.arange(QB)
    kr = np.arange(128)
    for j in range(NB):
        E = EXT[j]
        g = slots[j]
        qpos = g * QB + qr
        for k in range(8):
            c = E - 8 + k
            pos = c // 4
            jj = pos // 2
            sbk = pos if jj < 2 else pos ^ 1
            kpos = sbk * QB + (c % 4) * 128 + kr
            bad = kpos[:, None] > qpos[None, :]
            m[j, k][bad] = MASKNEG
    return m.astype(ml_dtypes.bfloat16)


_NC_CACHE = {}


def kernel(**inputs):
    x = np.asarray(inputs["L0_attn_x"], dtype=np.float32)
    names = [
        "L0_attn_W_K", "L0_attn_W_Q", "L0_attn_W_V", "L0_attn_W_O",
        "L0_ffn_gate_w", "L0_ffn_up_w", "L0_ffn_down_w",
        "L0_ffn_gate_scale", "L0_ffn_up_scale", "L0_ffn_down_scale",
        "L1_attn_Wqkv", "L1_attn_bqkv", "L1_attn_Wout", "L1_attn_bout",
        "L1_ffn_gate_w", "L1_ffn_wd", "L1_ffn_bd", "L1_ffn_wu", "L1_ffn_bu",
        "L1_ffn_ws_d", "L1_ffn_bs_d", "L1_ffn_ws_u", "L1_ffn_bs_u",
    ]
    wmap = {n: np.ascontiguousarray(np.asarray(inputs[n], dtype=np.float32))
            for n in names}

    if PHASES not in _NC_CACHE:
        _NC_CACHE[PHASES] = build_program()
    nc = _NC_CACHE[PHASES]

    in_maps = []
    for core in range(8):
        s, cls = core // 2, core % 2
        slots = _core_slots(cls)
        xb = x[s].reshape(8, QB, D)[slots]
        x0fm = np.ascontiguousarray(xb.reshape(TPC, D).T)
        m = {"x0_fm": x0fm, "masks": _make_masks(cls)}
        m.update(wmap)
        in_maps.append(m)

    res = run_bass_kernel_spmd(nc, in_maps, core_ids=list(range(8)))

    out = np.empty((4, 4096, D), np.float32)
    for core in range(8):
        s, cls = core // 2, core % 2
        slots = _core_slots(cls)
        yfm = np.asarray(res.results[core]["y_fm"])
        yt = yfm.T.reshape(NB, QB, D)
        for j, g in enumerate(slots):
            out[s, g * QB:(g + 1) * QB, :] = yt[j]
    return out


# version 06 (final)
# speedup vs baseline: 1.3462x; 1.3462x over previous
import math
import numpy as np
from contextlib import ExitStack

import concourse.bass as bass
import concourse.mybir as mybir
from concourse.bass_utils import run_bass_kernel_spmd

F32 = mybir.dt.float32
BF16 = mybir.dt.bfloat16
FP8 = mybir.dt.float8e4
U8 = mybir.dt.uint8
AF = mybir.ActivationFunctionType
OP = mybir.AluOpType
AX = mybir.AxisListType

D = 2048
NCH = 16
TPC = 2048
NB = 4
QB = 512
EXT = [8, 16, 24, 32]
A_SLOTS = [0, 2, 5, 7]
B_SLOTS = [1, 3, 4, 6]
NH = 16
DFF0 = 6144
NI0 = 48
NEB = 32
NS = 4
MASKNEG = -1.0e30
SCALE = 1.0 / math.sqrt(128.0)
PHASES = 6

ENGINES = ("sync", "gpsimd", "scalar", "vector", "tensor")


class Tok:
    __slots__ = ("sem", "val")

    def __init__(self, sem, val):
        self.sem = sem
        self.val = val


class Plan:
    def __init__(self):
        self.q = {e: [] for e in ENGINES}
        self.cnt = {s: 0 for s in ("sq", "sg", "pe", "act", "dve", "gp", "cc")}
        self.track = {}
        self.pending_fence = {}

    def _bw(self, name):
        e = self.track.get(name)
        if not e:
            return []
        out = list(e["r"])
        if e["w"] is not None:
            out.append(e["w"])
        return out

    def _br(self, name):
        e = self.track.get(name)
        return [e["w"]] if e and e["w"] is not None else []

    def _setw(self, name, tok):
        self.track[name] = {"w": tok, "r": []}

    def _addr(self, name, tok):
        e = self.track.setdefault(name, {"w": None, "r": []})
        e["r"].append(tok)

    def barrier(self):
        return [Tok(s, v) for s, v in self.cnt.items() if v > 0]

    def fence(self):
        toks = self.barrier()
        for e in ENGINES:
            self.pending_fence[e] = toks

    def _push(self, eng, waits, fn, sem, incby):
        ws = list(t for t in waits if t is not None)
        if eng in self.pending_fence:
            ws = list(self.pending_fence.pop(eng)) + ws
        ws = tuple(ws)
        if sem is not None:
            self.cnt[sem] += incby
            tok = Tok(sem, self.cnt[sem])
        else:
            tok = None
        self.q[eng].append((ws, fn, sem, incby))
        return tok

    def dma(self, qeng, out, in_, w=(), r=(), extra=()):
        sem = "sq" if qeng == "sync" else "sg"
        waits = list(extra)
        for b in w:
            waits += self._bw(b)
        for b in r:
            waits += self._br(b)

        def fn(eng):
            return eng.dma_start(out=out, in_=in_)

        tok = self._push(qeng, waits, fn, sem, 16)
        for b in w:
            self._setw(b, tok)
        for b in r:
            self._addr(b, tok)
        return tok

    def op(self, ename, fname, *args, w=(), r=(), extra=(), grp=False, **kw):
        sem = {"tensor": "pe", "scalar": "act", "vector": "dve",
               "gpsimd": "gp"}[ename]
        waits = list(extra)
        for b in w:
            if not grp:
                waits += self._bw(b)
        for b in r:
            waits += self._br(b)

        def fn(eng):
            return getattr(eng, fname)(*args, **kw)

        tok = self._push(ename, waits, fn, sem, 1)
        for b in w:
            self._setw(b, tok)
        for b in r:
            self._addr(b, tok)
        return tok

    def collective(self, ins, outs, groups, extra=()):
        def fn(eng):
            return eng.collective_compute(
                "AllGather", OP.bypass, replica_groups=groups,
                ins=ins, outs=outs)

        return self._push("gpsimd", list(extra), fn, "cc", 1)

    def _dma_boundaries(self):
        bounds = {}
        for qeng, sem in (("sync", "sq"), ("gpsimd", "sg")):
            total = 0
            bs = [0]
            for ws, fn, s, incby in self.q[qeng]:
                if ws:
                    if bs[-1] != total:
                        bs.append(total)
                if s == sem:
                    total += incby
            if bs[-1] != total:
                bs.append(total)
            bounds[sem] = bs
        return bounds

    def emitter(self, sems):
        import bisect
        bounds = self._dma_boundaries()

        def rnd(sem, val):
            bs = bounds.get(sem)
            if bs is None:
                return val
            i = bisect.bisect_left(bs, val)
            return bs[i] if i < len(bs) else bs[-1]

        def mk(ename):
            own = {"sync": "sq", "gpsimd": "sg"}.get(ename)

            def body(eng):
                lastwait = {}
                total = 0
                for ws, fn, sem, incby in self.q[ename]:
                    if own is not None and ws and total > lastwait.get(own, -1):
                        eng.wait_ge(sems[own], total)
                        lastwait[own] = total
                    for t in ws:
                        v = rnd(t.sem, t.val)
                        if lastwait.get(t.sem, -1) >= v:
                            continue
                        eng.wait_ge(sems[t.sem], v)
                        lastwait[t.sem] = v
                    ins = fn(eng)
                    if sem is not None:
                        ins.then_inc(sems[sem], incby)
                        if sem == own:
                            total += incby
            return body
        return mk


def build_program():
    nc = bass.Bass("TRN2", target_bir_lowering=False)
    din = {}

    def dt_in(name, shape):
        din[name] = nc.dram_tensor(name, list(shape), F32, kind="ExternalInput")
        return din[name]

    x0 = dt_in("x0_fm", (NCH * 128, TPC))
    masks = nc.dram_tensor("masks", [NB, 8, 128, QB], BF16, kind="ExternalInput")
    L0WK = dt_in("L0_attn_W_K", (NH, 128, D))
    L0WQ = dt_in("L0_attn_W_Q", (NH, 128, D))
    L0WV = dt_in("L0_attn_W_V", (NH, 128, D))
    L0WO = dt_in("L0_attn_W_O", (D, D))
    L0GW = dt_in("L0_ffn_gate_w", (D, DFF0))
    L0UW = dt_in("L0_ffn_up_w", (D, DFF0))
    L0DW = dt_in("L0_ffn_down_w", (DFF0, D))
    L0GS = dt_in("L0_ffn_gate_scale", (DFF0,))
    L0US = dt_in("L0_ffn_up_scale", (DFF0,))
    L0DS = dt_in("L0_ffn_down_scale", (D,))
    L1WQKV = dt_in("L1_attn_Wqkv", (3 * D, D))
    L1BQKV = dt_in("L1_attn_bqkv", (3 * D,))
    L1WOUT = dt_in("L1_attn_Wout", (D, D))
    L1BOUT = dt_in("L1_attn_bout", (D,))
    L1GW = dt_in("L1_ffn_gate_w", (8, D))
    L1WD = dt_in("L1_ffn_wd", (8, 512, D))
    L1BD = dt_in("L1_ffn_bd", (8, 512))
    L1WU = dt_in("L1_ffn_wu", (8, D, 512))
    L1BU = dt_in("L1_ffn_bu", (8, D))
    L1WSD = dt_in("L1_ffn_ws_d", (512, D))
    L1BSD = dt_in("L1_ffn_bs_d", (512,))
    L1WSU = dt_in("L1_ffn_ws_u", (D, 512))
    L1BSU = dt_in("L1_ffn_bs_u", (D,))

    yout = nc.dram_tensor("y_fm", [NCH * 128, TPC], F32, kind="ExternalOutput")

    xsd = nc.dram_tensor("x_scr", [NCH, 128, TPC], F32)
    gu8s = nc.dram_tensor("gu8s", [NI0, 128, 2 * D], U8)
    d8s = nc.dram_tensor("d8s", [NI0, 128, D], U8)
    qshd = nc.dram_tensor("qsh", [NH, 128, TPC], BF16)
    qsld = nc.dram_tensor("qsl", [NH, 128, TPC], BF16)
    kvown = nc.dram_tensor("kv_own", [NB, 4, NH, 128, QB], BF16)
    kvg = nc.dram_tensor("kv_g", [2 * NB, 4, NH, 128, QB], BF16)
    aohd = nc.dram_tensor("aoh", [NH, 128, TPC], BF16)
    aold = nc.dram_tensor("aol", [NH, 128, TPC], BF16)
    wqTh = nc.dram_tensor("wqTh", [NI0, 128, D], BF16)
    wqTl = nc.dram_tensor("wqTl", [NI0, 128, D], BF16)
    woTh = nc.dram_tensor("woTh", [NCH, 128, D], BF16)
    woTl = nc.dram_tensor("woTl", [NCH, 128, D], BF16)
    wdTh = nc.dram_tensor("wdTh", [NEB, 128, D], BF16)
    wdTl = nc.dram_tensor("wdTl", [NEB, 128, D], BF16)
    wuTh = nc.dram_tensor("wuTh", [NCH, 128, NEB * 128], BF16)
    wuTl = nc.dram_tensor("wuTl", [NCH, 128, NEB * 128], BF16)
    wsdTh = nc.dram_tensor("wsdTh", [NS, 128, D], BF16)
    wsdTl = nc.dram_tensor("wsdTl", [NS, 128, D], BF16)
    wsuTh = nc.dram_tensor("wsuTh", [NCH, 128, 512], BF16)
    wsuTl = nc.dram_tensor("wsuTl", [NCH, 128, 512], BF16)

    p = Plan()

    with ExitStack() as ctx:
        def sb(name, shape, dt):
            return ctx.enter_context(nc.sbuf_tensor(name, list(shape), dt))

        sems = {}
        for s in ("sq", "sg", "pe", "act", "dve", "gp", "cc"):
            sems[s] = ctx.enter_context(nc.semaphore(s))

        ident = sb("ident", (128, 128), F32)
        ones128 = sb("ones128", (128, 1), F32)
        ones1 = sb("ones1", (1, 128), F32)
        ones128b = sb("ones128b", (128, 1), BF16)
        consts = sb("consts", (128, 2), F32)
        nc.const_aps.register(F32, 0.0, consts[:, 0:1])
        nc.const_aps.register(F32, 1.0e-5, consts[:, 1:2])
        gs_sb = sb("gs_sb", (128, NI0), F32)
        us_sb = sb("us_sb", (128, NI0), F32)
        dsh_sb = sb("dsh_sb", (128, NCH), F32)
        bqkv_sb = sb("bqkv_sb", (128, NI0), F32)
        bout_sb = sb("bout_sb", (128, NCH), F32)
        bd_sb = sb("bd_sb", (128, NEB), F32)
        bu_sb = sb("bu_sb", (8, D), F32)
        bsd_sb = sb("bsd_sb", (128, NS), F32)
        bsuh_sb = sb("bsuh_sb", (128, NCH), F32)
        gTh_sb = sb("gTh_sb", (128, NCH * 8), BF16)
        gTl_sb = sb("gTl_sb", (128, NCH * 8), BF16)
        rows = sb("rows", (1, 6 * QB), F32)

        pb = [ctx.enter_context(nc.psum_tensor(f"pb{i}", [128, 512], F32))
              for i in range(8)]
        PB = [f"pb{i}" for i in range(8)]

        big1 = sb("big1", (128, NI0 * QB), BF16)
        big2 = sb("big2", (128, NI0 * QB), BF16)
        af1 = big1.ap().bitcast(F32)
        xlnh = sb("xlnh", (128, NCH * QB), BF16)
        xlnl = sb("xlnl", (128, NCH * QB), BF16)
        wbh = sb("wbh", (128, D), BF16)
        wbl = sb("wbl", (128, D), BF16)
        wbh2 = sb("wbh2", (128, D), BF16)
        wbl2 = sb("wbl2", (128, D), BF16)
        w8t = sb("w8t", (128, 4 * D), U8)
        tmp = [sb(f"tmp{i}", (128, 512), F32) for i in range(7)]
        TN = [f"tmp{i}" for i in range(7)]
        xbb = [sb(f"xbb{i}", (128, 4 * 512), F32) for i in range(2)]

        def tb(i):
            return tmp[i], TN[i]

        def XK(c, t):
            return f"X{c}_{t}"

        p.op("gpsimd", "memset", consts[:, 0:1], 0.0, w=["consts"])
        p.op("gpsimd", "memset", consts[:, 1:2], 1.0e-5, w=["consts"],
             r=["consts"])
        t0 = p.op("gpsimd", "memset", ident[:, :], 0.0, w=["ident"])
        p.op("gpsimd", "affine_select", out=ident[:, :], in_=ident[:, :],
             compare_op=OP.not_equal, fill=1.0, base=0,
             pattern=[[-1, 128]], channel_multiplier=1,
             w=["ident"], extra=[t0])
        p.op("gpsimd", "memset", ones128[:, :], 1.0, w=["ones128"])
        p.op("gpsimd", "memset", ones128b[:, :], 1.0, w=["ones128b"])
        p.op("gpsimd", "memset", ones1[:, :], 1.0, w=["ones1"])

        def load_cvec(dst, dname, src, ncols):
            def fn(eng):
                return eng.dma_start(
                    out=dst[:, 0:ncols],
                    in_=bass.AP(src, 0, [[1, 128], [128, ncols]]),
                    allow_slow_non_contiguous=True)
            tok = p._push("sync", p._bw(dname), fn, "sq", 16)
            p._setw(dname, tok)
            return tok

        load_cvec(gs_sb, "gs", L0GS, NI0)
        load_cvec(us_sb, "us", L0US, NI0)
        load_cvec(dsh_sb, "dsh", L0DS, NCH)
        load_cvec(bqkv_sb, "bqkv", L1BQKV, NI0)
        load_cvec(bout_sb, "bout", L1BOUT, NCH)
        load_cvec(bd_sb, "bd", L1BD, NEB)
        load_cvec(bsd_sb, "bsd", L1BSD, NS)
        load_cvec(bsuh_sb, "bsuh", L1BSU, NCH)
        p.dma("sync", bu_sb[0:8, :], L1BU[:, :], w=["bu"])
        p.op("vector", "tensor_scalar", out=dsh_sb[:, 0:NCH],
             in0=dsh_sb[:, 0:NCH], scalar1=0.5, scalar2=None, op0=OP.mult,
             w=["dsh"], r=["dsh"])
        p.op("vector", "tensor_scalar", out=bsuh_sb[:, 0:NCH],
             in0=bsuh_sb[:, 0:NCH], scalar1=0.5, scalar2=None, op0=OP.mult,
             w=["bsuh"], r=["bsuh"])

        def split(srcap, srcbuf, hiap, hibuf, pdi, loap, lobuf, width):
            pd, pdn = tb(pdi)
            p.op("scalar", "copy", out=hiap, in_=srcap,
                 w=[hibuf], r=[srcbuf])
            p.op("vector", "tensor_sub", pd[:, 0:width], srcap, hiap,
                 w=[pdn], r=[srcbuf, hibuf])
            p.op("vector", "tensor_copy", out=loap, in_=pd[:, 0:width],
                 w=[lobuf], r=[pdn])

        def tstash(ncols, dsth, dstl, dkey):
            nT = ncols // 128
            for c in range(nT):
                bank = PB[c % 2]
                p.op("tensor", "transpose", pb[c % 2][:, 0:128],
                     af1[:, c * 128:(c + 1) * 128], ident[:, :],
                     w=[bank], r=["wsl", "ident"])
                csl = slice(c * 128, (c + 1) * 128)
                p.op("scalar", "copy", out=wbh[:, csl],
                     in_=pb[c % 2][:, 0:128], w=["wbh"], r=[bank],
                     grp=(c > 0))
                pd, pdn = tb(6)
                p.op("vector", "tensor_sub", pd[:, 0:128],
                     pb[c % 2][:, 0:128], wbh[:, csl],
                     w=[pdn], r=[bank, "wbh"])
                p.op("vector", "tensor_copy", out=wbl[:, csl],
                     in_=pd[:, 0:128], w=["wbl"], r=[pdn], grp=(c > 0))
            p.dma("gpsimd", dsth[:, 0:ncols], wbh[:, 0:ncols], w=[dkey],
                  r=["wbh"])
            p.dma("gpsimd", dstl[:, 0:ncols], wbl[:, 0:ncols], w=[dkey],
                  r=["wbl"])

        def ln_tile(t, src, srckeys, dsth, dstl):
            for c in range(NCH):
                g, ci = c // 4, c % 4
                xb, xn = xbb[g % 2], f"xbb{g % 2}"
                if ci == 0:
                    p.dma("sync", xb[:, :], src4(src, c), w=[xn],
                          r=[srckeys(c + k) for k in range(4)])
                xbc = xb[:, ci * QB:(ci + 1) * QB]
                sq, sqn = tb(c % 2)
                p.op("scalar", "square", out=sq[:, :], in_=xbc,
                     w=[sqn], r=[xn])
                p.op("tensor", "matmul", pb[6][0:1, :], ones128[:, :],
                     xbc, start=(c == 0), stop=(c == NCH - 1),
                     w=[PB[6]], r=[xn, "ones128"], grp=(c > 0))
                p.op("tensor", "matmul", pb[7][0:1, :], ones128[:, :],
                     sq[:, :], start=(c == 0), stop=(c == NCH - 1),
                     w=[PB[7]], r=[sqn, "ones128"], grp=(c > 0))
            mean = rows[0:1, 0:QB]
            ex2 = rows[0:1, QB:2 * QB]
            var = rows[0:1, 2 * QB:3 * QB]
            rstd = rows[0:1, 3 * QB:4 * QB]
            mrs = rows[0:1, 4 * QB:5 * QB]
            p.op("vector", "tensor_scalar", out=mean, in0=pb[6][0:1, :],
                 scalar1=1.0 / D, scalar2=None, op0=OP.mult,
                 w=["rows"], r=[PB[6], "rows"])
            p.op("vector", "tensor_scalar", out=ex2, in0=pb[7][0:1, :],
                 scalar1=1.0 / D, scalar2=None, op0=OP.mult,
                 w=["rows"], r=[PB[7], "rows"])
            p.op("vector", "tensor_mul", var, mean, mean, w=["rows"], r=["rows"])
            p.op("vector", "tensor_sub", var, ex2, var, w=["rows"], r=["rows"])
            p.op("scalar", "activation", out=rstd, in_=var, func=AF.Sqrt,
                 bias=1.0e-5, scale=1.0, w=["rows"], r=["rows"])
            p.op("vector", "reciprocal", out=rstd, in_=rstd,
                 w=["rows"], r=["rows"])
            p.op("vector", "tensor_mul", mrs, mean, rstd, w=["rows"], r=["rows"])
            p.op("tensor", "matmul", pb[4][:, :], ones1[:, :], rstd,
                 start=True, stop=True, w=[PB[4]], r=["rows", "ones1"])
            p.op("tensor", "matmul", pb[5][:, :], ones1[:, :], mrs,
                 start=True, stop=True, w=[PB[5]], r=["rows", "ones1"])
            for c in range(NCH):
                g, ci = c // 4, c % 4
                xb, xn = xbb[g % 2], f"xbb{g % 2}"
                if ci == 0:
                    p.dma("sync", xb[:, :], src4(src, c), w=[xn],
                          r=[srckeys(c + k) for k in range(4)])
                xbc = xb[:, ci * QB:(ci + 1) * QB]
                v32, vn = tb(2 + c % 2)
                p.op("vector", "tensor_mul", v32[:, :], xbc, pb[4][:, :],
                     w=[vn], r=[xn, PB[4]])
                p.op("vector", "tensor_sub", v32[:, :], v32[:, :], pb[5][:, :],
                     w=[vn], r=[vn, PB[5]])
                hap, hbuf = dsth(c)
                lap, lbuf = dstl(c)
                split(v32[:, :], vn, hap, hbuf, 4 + c % 2, lap, lbuf, QB)

        def src4(src, c0):
            a0 = src(c0)
            a1 = src(c0 + 3)
            st = (a1.offset - a0.offset) // 3
            ap0 = [list(x) for x in a0.ap]
            return bass.AP(a0.tensor, a0.offset,
                           [ap0[0], [st, 4], ap0[1]])

        def xsrc(sub):
            if sub == 0:
                return lambda t: (lambda c: x0[c * 128:(c + 1) * 128,
                                               t * QB:(t + 1) * QB])
            return lambda t: (lambda c: xsd[c, :, t * QB:(t + 1) * QB])

        def xdst(sub):
            if sub == PHASES - 1:
                return lambda c, t: yout[c * 128:(c + 1) * 128,
                                         t * QB:(t + 1) * QB]
            return lambda c, t: xsd[c, :, t * QB:(t + 1) * QB]

        def xkeys(sub):
            if sub == 0:
                return lambda t: (lambda c: "x0ro")
            return lambda t: (lambda c: XK(c, t))

        def l0_quant():
            p.fence()
            for (wsrc, part2, skey, nrch, ncol) in (
                    (L0GW, 0, "G8", NCH, DFF0),
                    (L0UW, 1, "G8", NCH, DFF0),
                    (L0DW, 2, "D8", NI0, D)):
                for rch in range(nrch):
                    p.dma("sync", af1[:, 0:ncol],
                          wsrc[rch * 128:(rch + 1) * 128, :], w=["wsl"])
                    for piece in range((ncol + 2047) // 2048):
                        lo = piece * 2048
                        hi = min(ncol, lo + 2048)
                        npc = hi - lo
                        p.op("scalar", "copy",
                             out=w8t.ap().bitcast(FP8)[:, 0:npc],
                             in_=af1[:, lo:hi], w=["w8t"], r=["wsl"])
                        ni = npc // 128
                        if part2 == 2:
                            dstap = d8s[rch, :, piece * 2048:piece * 2048 + npc]
                        else:
                            off = part2 * D + rch * 128
                            base = (piece * 16) * (128 * 2 * D) + off
                            dstap = bass.AP(gu8s, base,
                                            [[2 * D, 128], [128 * 2 * D, ni],
                                             [1, 128]])
                        p.dma("gpsimd", dstap, w8t[:, 0:npc],
                              w=[skey], r=["w8t"])

        def l0_ffn(sub):
            p.fence()
            src = xsrc(sub)
            dst = xdst(sub)
            keys = xkeys(sub)
            for t in range(NB):
                ln_tile(t, src(t), keys(t),
                        lambda c: (xlnh[:, c * QB:(c + 1) * QB], "XH"),
                        lambda c: (xlnl[:, c * QB:(c + 1) * QB], "XL"))
                for i in range(NI0):
                    par = i % 2
                    w8off = par * 2 * D
                    gslab = wbh if par == 0 else wbl
                    gslabn = "wbh" if par == 0 else "wbl"
                    uslab = wbh2 if par == 0 else wbl2
                    uslabn = "wbh2" if par == 0 else "wbl2"
                    p.dma("sync", w8t[:, w8off:w8off + 2 * D],
                          gu8s[i, :, :], w=[f"w8t{par}"], r=["G8"])
                    p.op("scalar", "copy", out=gslab[:, :],
                         in_=w8t.ap().bitcast(FP8)[:, w8off:w8off + D],
                         w=[gslabn], r=[f"w8t{par}"])
                    p.op("scalar", "copy", out=uslab[:, :],
                         in_=w8t.ap().bitcast(FP8)[:, w8off + D:w8off + 2 * D],
                         w=[uslabn], r=[f"w8t{par}"])
                    gb, ub = 0 + par * 2, 1 + par * 2
                    for c in range(NCH):
                        cs = slice(c * QB, (c + 1) * QB)
                        p.op("tensor", "matmul", pb[gb][:, :],
                             gslab[:, c * 128:(c + 1) * 128], xlnh[:, cs],
                             start=(c == 0), stop=False, w=[PB[gb]],
                             r=[gslabn, "XH"], grp=(c > 0))
                        p.op("tensor", "matmul", pb[gb][:, :],
                             gslab[:, c * 128:(c + 1) * 128], xlnl[:, cs],
                             start=False, stop=(c == NCH - 1),
                             w=[PB[gb]], r=["XL", gslabn], grp=True)
                    for c in range(NCH):
                        cs = slice(c * QB, (c + 1) * QB)
                        p.op("tensor", "matmul", pb[ub][:, :],
                             uslab[:, c * 128:(c + 1) * 128], xlnh[:, cs],
                             start=(c == 0), stop=False, w=[PB[ub]],
                             r=[uslabn, "XH"], grp=(c > 0))
                        p.op("tensor", "matmul", pb[ub][:, :],
                             uslab[:, c * 128:(c + 1) * 128], xlnl[:, cs],
                             start=False, stop=(c == NCH - 1),
                             w=[PB[ub]], r=["XL", uslabn], grp=True)
                    ta, tan = tb(0)
                    tu, tun = tb(1)
                    tc, tcn = tb(2)
                    p.op("scalar", "activation", out=ta[:, :],
                         in_=pb[gb][:, :], func=AF.Silu, bias=0.0,
                         scale=gs_sb[:, i:i + 1], w=[tan], r=[PB[gb], "gs"])
                    p.op("vector", "tensor_scalar", out=tu[:, :],
                         in0=pb[ub][:, :], scalar1=us_sb[:, i:i + 1],
                         scalar2=None, op0=OP.mult, w=[tun],
                         r=[PB[ub], "us"])
                    p.op("vector", "tensor_mul", tc[:, :], ta[:, :], tu[:, :],
                         w=[tcn], r=[tan, tun])
                    isl = slice(i * QB, (i + 1) * QB)
                    split(tc[:, :], tcn, big1[:, isl], "FH", 3,
                          big2[:, isl], "FL", QB)
                for rnd in range(2):
                    for i in range(NI0):
                        par = i % 2
                        w8off = par * 1024
                        dslab = wbh2 if par == 0 else wbl2
                        dslabn = "wbh2" if par == 0 else "wbl2"
                        p.dma("sync", w8t[:, w8off:w8off + 1024],
                              d8s[i, :, rnd * 1024:(rnd + 1) * 1024],
                              w=[f"w8t{par}"], r=["D8"])
                        p.op("scalar", "copy", out=dslab[:, 0:1024],
                             in_=w8t.ap().bitcast(FP8)[:, w8off:w8off + 1024],
                             w=[dslabn], r=[f"w8t{par}"])
                        isl = slice(i * QB, (i + 1) * QB)
                        for o in range(8):
                            p.op("tensor", "matmul", pb[o][:, :],
                                 dslab[:, o * 128:(o + 1) * 128], big1[:, isl],
                                 start=(i == 0), stop=False, w=[PB[o]],
                                 r=[dslabn, "FH"], grp=(i > 0))
                            p.op("tensor", "matmul", pb[o][:, :],
                                 dslab[:, o * 128:(o + 1) * 128], big2[:, isl],
                                 start=False, stop=(i == NI0 - 1), w=[PB[o]],
                                 r=["FL", dslabn], grp=True)
                    for g in range(2):
                        c0 = rnd * 8 + g * 4
                        xb, xn = xbb[g], f"xbb{g}"
                        p.dma("sync", xb[:, :], src4(src(t), c0), w=[xn],
                              r=[keys(t)(c0 + k) for k in range(4)])
                    for o in range(8):
                        oc = rnd * 8 + o
                        ta, tan = tb(o % 2)
                        tc, tcn = tb(2 + o % 2)
                        p.op("scalar", "activation", out=ta[:, :],
                             in_=pb[o][:, :], func=AF.Copy, bias=0.0,
                             scale=dsh_sb[:, oc:oc + 1], w=[tan],
                             r=[PB[o], "dsh"])
                        xb, xn = xbb[o // 4], f"xbb{o // 4}"
                        p.op("vector", "tensor_add", tc[:, :], ta[:, :],
                             xb[:, (o % 4) * QB:(o % 4 + 1) * QB],
                             w=[tcn], r=[tan, xn])
                        p.dma("gpsimd", dst(oc, t), tc[:, :],
                              w=[XK(oc, t)], r=[tcn])

        def prep_qkv_weights(layer):
            if layer == 0:
                srcs = [(role * NH + h, wsrc[h, :, :])
                        for role, wsrc in ((0, L0WQ), (1, L0WK), (2, L0WV))
                        for h in range(NH)]
            else:
                srcs = [(o, L1WQKV[o * 128:(o + 1) * 128, :])
                        for o in range(NI0)]
            for o, srcap in srcs:
                p.dma("sync", af1[:, 0:D], srcap, w=["wsl"])
                tstash(D, wqTh[o, :, :], wqTl[o, :, :], "WQT")

        def prep_wo_weights(layer):
            wsrc = L0WO if layer == 0 else L1WOUT
            for dd in range(NCH):
                p.dma("sync", af1[:, 0:D], wsrc[dd * 128:(dd + 1) * 128, :],
                      w=["wsl"])
                tstash(D, woTh[dd, :, :], woTl[dd, :, :], "WOT")

        def attention(sub, layer):
            p.fence()
            src = xsrc(sub)
            dst = xdst(sub)
            keys = xkeys(sub)
            prep_qkv_weights(layer)
            prep_wo_weights(layer)
            p.fence()

            def xh_ap(c, t):
                if c < 12:
                    return big1[:, c * 2048 + t * QB: c * 2048 + t * QB + QB]
                return xlnh[:, (c - 12) * 2048 + t * QB:
                            (c - 12) * 2048 + t * QB + QB]

            def xl_ap(c, t):
                if c < 12:
                    return big2[:, c * 2048 + t * QB: c * 2048 + t * QB + QB]
                return xlnl[:, (c - 12) * 2048 + t * QB:
                            (c - 12) * 2048 + t * QB + QB]

            for t in range(NB):
                ln_tile(t, src(t), keys(t),
                        lambda c, t=t: (xh_ap(c, t), "XH"),
                        lambda c, t=t: (xl_ap(c, t), "XL"))

            for o in range(NI0):
                role, h = o // NH, o % NH
                par = o % 2
                sh = wbh if par == 0 else wbh2
                shn = "wbh" if par == 0 else "wbh2"
                sl = wbl if par == 0 else wbl2
                sln = "wbl" if par == 0 else "wbl2"
                p.dma("sync", sh[:, :], wqTh[o, :, :], w=[shn], r=["WQT"])
                p.dma("sync", sl[:, :], wqTl[o, :, :], w=[sln], r=["WQT"])
                for t in range(NB):
                    bank = PB[4 + t % 2]
                    bk = pb[4 + t % 2]
                    for c in range(NCH):
                        p.op("tensor", "matmul", bk[:, :],
                             sh[:, c * 128:(c + 1) * 128], xh_ap(c, t),
                             start=(c == 0), stop=False, w=[bank],
                             r=[shn, "XH"], grp=(c > 0))
                        p.op("tensor", "matmul", bk[:, :],
                             sl[:, c * 128:(c + 1) * 128], xh_ap(c, t),
                             start=False, stop=False, w=[bank],
                             r=[sln, "XH"], grp=True)
                        p.op("tensor", "matmul", bk[:, :],
                             sh[:, c * 128:(c + 1) * 128], xl_ap(c, t),
                             start=False, stop=(c == NCH - 1), w=[bank],
                             r=["XL", shn], grp=True)
                    ta, tan = tb(0)
                    if layer == 1:
                        p.op("vector", "tensor_scalar", out=ta[:, :],
                             in0=bk[:, :], scalar1=bqkv_sb[:, o:o + 1],
                             scalar2=None, op0=OP.add,
                             w=[tan], r=[bank, "bqkv"])
                    else:
                        p.op("vector", "tensor_copy", out=ta[:, :],
                             in_=bk[:, :], w=[tan], r=[bank])
                    if role in (0, 1):
                        hi = (qshd[h, :, t * QB:(t + 1) * QB] if role == 0
                              else kvown[t, 0, h, :, :])
                        lo = (qsld[h, :, t * QB:(t + 1) * QB] if role == 0
                              else kvown[t, 1, h, :, :])
                        hb = tmp[1].ap().bitcast(BF16)[:, 0:QB]
                        lb = tmp[1].ap().bitcast(BF16)[:, QB:2 * QB]
                        split(ta[:, :], tan, hb, "tmp1", 2, lb, "tmp1", QB)
                        p.dma("gpsimd", hi, hb, w=["QS"], r=["tmp1"])
                        p.dma("gpsimd", lo, lb, w=["QS"], r=["tmp1"])
                    else:
                        hb = tmp[1].ap().bitcast(BF16)
                        for q4 in range(4):
                            bank2 = PB[6 + q4 % 2]
                            p.op("tensor", "transpose",
                                 pb[6 + q4 % 2][:, 0:128],
                                 ta[:, q4 * 128:(q4 + 1) * 128],
                                 ident[:, :], w=[bank2], r=[tan, "ident"])
                            qs2 = slice(q4 * 128, (q4 + 1) * 128)
                            p.op("scalar", "copy", out=hb[:, qs2],
                                 in_=pb[6 + q4 % 2][:, 0:128],
                                 w=["tmp1"], r=[bank2])
                            pd, pdn = tb(2)
                            p.op("vector", "tensor_sub", pd[:, 0:128],
                                 pb[6 + q4 % 2][:, 0:128], hb[:, qs2],
                                 w=[pdn], r=[bank2, "tmp1"])
                            p.op("vector", "tensor_copy",
                                 out=hb[:, QB + q4 * 128:QB + (q4 + 1) * 128],
                                 in_=pd[:, 0:128], w=["tmp1"], r=[pdn])
                        p.dma("gpsimd", kvown[t, 2, h, :, :], hb[:, 0:QB],
                              w=["QS"], r=["tmp1"])
                        p.dma("gpsimd", kvown[t, 3, h, :, :],
                              hb[:, QB:2 * QB], w=["QS"], r=["tmp1"])

            groups = [[0, 1], [2, 3], [4, 5], [6, 7]]
            bar2 = p.barrier()
            cc_toks = []
            for j in range(NB):
                cc_toks.append(p.collective([kvown[j]],
                                            [kvg[2 * j:2 * j + 2]],
                                            groups, extra=bar2))

            p.fence()
            for j in range(NB):
                for k in range(8):
                    p.dma("sync",
                          big2[:, (j * 8 + k) * QB:(j * 8 + k + 1) * QB],
                          masks[j, k, :, :], w=["MK"])
            for h in range(NH):
                for kind in range(4):
                    srcv = kvg[:, kind, h, :, :].transpose([1, 0, 2])
                    p.dma("sync", big1[:, kind * 4096:(kind + 1) * 4096],
                          srcv, w=["KV"], extra=list(cc_toks))
                for j in range(NB):
                    E = EXT[j]
                    qhb = tmp[5].ap().bitcast(BF16)[:, 0:QB]
                    qlb = tmp[5].ap().bitcast(BF16)[:, QB:2 * QB]
                    p.dma("sync", qhb, qshd[h, :, j * QB:(j + 1) * QB],
                          w=["tmp5"], r=["QS"])
                    p.dma("sync", qlb, qsld[h, :, j * QB:(j + 1) * QB],
                          w=["tmp5"], r=["QS"])
                    SROT = [0, 1, 5]

                    def emit_qk(c):
                        sbank = PB[SROT[c % 3]]
                        sp = pb[SROT[c % 3]]
                        kh_ap = big1[:, c * 128:(c + 1) * 128]
                        kl_ap = big1[:, 4096 + c * 128:4096 + (c + 1) * 128]
                        p.op("tensor", "matmul", sp[:, :], kh_ap, qhb,
                             start=True, stop=False, w=[sbank],
                             r=["KV", "tmp5"])
                        p.op("tensor", "matmul", sp[:, :], kl_ap, qhb,
                             start=False, stop=False, w=[sbank],
                             r=["KV", "tmp5"], grp=True)
                        p.op("tensor", "matmul", sp[:, :], kh_ap, qlb,
                             start=False, stop=True, w=[sbank],
                             r=["KV", "tmp5"], grp=True)

                    def emit_evac(c):
                        sbank = PB[SROT[c % 3]]
                        sp = pb[SROT[c % 3]]
                        vh_ap = big1[:, 8192 + c * 128:8192 + (c + 1) * 128]
                        vl_ap = big1[:, 12288 + c * 128:12288 + (c + 1) * 128]
                        if c >= E - 8:
                            k = c - (E - 8)
                            msl = slice((j * 8 + k) * QB, (j * 8 + k + 1) * QB)
                            p.op("vector", "tensor_add", sp[:, :], sp[:, :],
                                 big2[:, msl], w=[sbank], r=[sbank, "MK"])
                        pa, pan = tb(c % 2)
                        pp = tmp[2 + c % 2]
                        ppn = TN[2 + c % 2]
                        phb = pp.ap().bitcast(BF16)[:, 0:QB]
                        plb = pp.ap().bitcast(BF16)[:, QB:2 * QB]
                        p.op("scalar", "activation", out=pa[:, :],
                             in_=sp[:, :], func=AF.Exp, bias=0.0, scale=SCALE,
                             w=[pan], r=[sbank])
                        p.op("scalar", "copy", out=phb, in_=pa[:, :],
                             w=[ppn], r=[pan])
                        pd, pdn = tb(4)
                        p.op("vector", "tensor_sub", pd[:, :], pa[:, :], phb,
                             w=[pdn], r=[pan, ppn])
                        p.op("vector", "tensor_copy", out=plb, in_=pd[:, :],
                             w=[ppn], r=[pdn])
                        p.op("tensor", "matmul", pb[3][0:1, :],
                             ones128b[:, :], phb, start=(c == 0), stop=False,
                             w=[PB[3]], r=[ppn, "ones128b"], grp=(c > 0))
                        p.op("tensor", "matmul", pb[3][0:1, :],
                             ones128b[:, :], plb, start=False,
                             stop=(c == E - 1), w=[PB[3]],
                             r=[ppn, "ones128b"], grp=True)
                        p.op("tensor", "matmul", pb[2][:, :], vh_ap, phb,
                             start=(c == 0), stop=False, w=[PB[2]],
                             r=["KV", ppn], grp=(c > 0))
                        p.op("tensor", "matmul", pb[2][:, :], vl_ap, phb,
                             start=False, stop=False, w=[PB[2]],
                             r=["KV", ppn], grp=True)
                        p.op("tensor", "matmul", pb[2][:, :], vh_ap, plb,
                             start=False, stop=(c == E - 1), w=[PB[2]],
                             r=["KV", ppn], grp=True)

                    for c in range(E):
                        emit_qk(c)
                        if c >= 1:
                            emit_evac(c - 1)
                    emit_evac(E - 1)
                    sr = rows[0:1, 5 * QB:6 * QB]
                    p.op("vector", "tensor_copy", out=sr, in_=pb[3][0:1, :],
                         w=["rows"], r=[PB[3], "rows"])
                    p.op("vector", "reciprocal", out=sr, in_=sr,
                         w=["rows"], r=["rows"])
                    p.op("tensor", "matmul", pb[4][:, :], ones1[:, :], sr,
                         start=True, stop=True, w=[PB[4]], r=["rows", "ones1"])
                    rb, rbn = tb(1)
                    p.op("vector", "tensor_copy", out=rb[:, :],
                         in_=pb[4][:, :], w=[rbn], r=[PB[4]])
                    ta, tan = tb(6)
                    p.op("vector", "tensor_mul", ta[:, :], pb[2][:, :],
                         rb[:, :], w=[tan], r=[PB[2], rbn])
                    ob = tmp[5].ap().bitcast(BF16)
                    p.op("scalar", "copy", out=ob[:, 0:QB], in_=ta[:, :],
                         w=["tmp5"], r=[tan])
                    pd, pdn = tb(4)
                    p.op("vector", "tensor_sub", pd[:, :], ta[:, :],
                         ob[:, 0:QB], w=[pdn], r=[tan, "tmp5"])
                    p.op("vector", "tensor_copy", out=ob[:, QB:2 * QB],
                         in_=pd[:, :], w=["tmp5"], r=[pdn])
                    p.dma("gpsimd", aohd[h, :, j * QB:(j + 1) * QB],
                          ob[:, 0:QB], w=["AO"], r=["tmp5"])
                    p.dma("gpsimd", aold[h, :, j * QB:(j + 1) * QB],
                          ob[:, QB:2 * QB], w=["AO"], r=["tmp5"])

            p.fence()
            for t in range(NB):
                for f in range(NCH):
                    p.dma("sync", xlnh[:, f * QB:(f + 1) * QB],
                          aohd[f, :, t * QB:(t + 1) * QB], w=["XH"], r=["AO"])
                    p.dma("sync", xlnl[:, f * QB:(f + 1) * QB],
                          aold[f, :, t * QB:(t + 1) * QB], w=["XL"], r=["AO"])
                for dd in range(NCH):
                    if dd % 4 == 0:
                        xbg, xgn = xbb[(dd // 4) % 2], f"xbb{(dd // 4) % 2}"
                        p.dma("sync", xbg[:, :], src4(src(t), dd), w=[xgn],
                              r=[keys(t)(dd + k) for k in range(4)])
                    par = dd % 2
                    sh = wbh if par == 0 else wbh2
                    shn = "wbh" if par == 0 else "wbh2"
                    sl = wbl if par == 0 else wbl2
                    sln = "wbl" if par == 0 else "wbl2"
                    p.dma("sync", sh[:, :], woTh[dd, :, :], w=[shn],
                          r=["WOT"])
                    p.dma("sync", sl[:, :], woTl[dd, :, :], w=[sln],
                          r=["WOT"])
                    bank = PB[4 + dd % 2]
                    bk = pb[4 + dd % 2]
                    for f in range(NCH):
                        fs = slice(f * QB, (f + 1) * QB)
                        p.op("tensor", "matmul", bk[:, :],
                             sh[:, f * 128:(f + 1) * 128], xlnh[:, fs],
                             start=(f == 0), stop=False, w=[bank],
                             r=[shn, "XH"], grp=(f > 0))
                        p.op("tensor", "matmul", bk[:, :],
                             sl[:, f * 128:(f + 1) * 128], xlnh[:, fs],
                             start=False, stop=False, w=[bank],
                             r=[sln, "XH"], grp=True)
                        p.op("tensor", "matmul", bk[:, :],
                             sh[:, f * 128:(f + 1) * 128], xlnl[:, fs],
                             start=False, stop=(f == NCH - 1), w=[bank],
                             r=["XL", shn], grp=True)
                    ta, tan = tb(0 + dd % 2)
                    if layer == 1:
                        p.op("vector", "tensor_scalar", out=ta[:, :],
                             in0=bk[:, :], scalar1=bout_sb[:, dd:dd + 1],
                             scalar2=None, op0=OP.add,
                             w=[tan], r=[bank, "bout"])
                    else:
                        p.op("vector", "tensor_copy", out=ta[:, :],
                             in_=bk[:, :], w=[tan], r=[bank])
                    xb, xn = xbb[(dd // 4) % 2], f"xbb{(dd // 4) % 2}"
                    tc, tcn = tb(2 + dd % 2)
                    p.op("vector", "tensor_add", tc[:, :], ta[:, :],
                         xb[:, (dd % 4) * QB:(dd % 4 + 1) * QB],
                         w=[tcn], r=[tan, xn])
                    p.dma("gpsimd", dst(dd, t), tc[:, :],
                          w=[XK(dd, t)], r=[tcn])

        def moe_prep():
            p.fence()
            wdflat = L1WD.reshape([8 * 512, D])
            for eb in range(NEB):
                p.dma("sync", af1[:, 0:D],
                      wdflat[eb * 128:(eb + 1) * 128, :], w=["wsl"])
                tstash(D, wdTh[eb, :, :], wdTl[eb, :, :], "WDT")
            for e in range(8):
                for dc in range(NCH):
                    p.dma("sync", af1[:, 0:512],
                          L1WU[e, dc * 128:(dc + 1) * 128, :], w=["wsl"])
                    for b4 in range(4):
                        bank = PB[b4 % 2]
                        p.op("tensor", "transpose", pb[b4 % 2][:, 0:128],
                             af1[:, b4 * 128:(b4 + 1) * 128], ident[:, :],
                             w=[bank], r=["wsl", "ident"])
                        csl = slice(b4 * 128, (b4 + 1) * 128)
                        p.op("scalar", "copy", out=wbh[:, csl],
                             in_=pb[b4 % 2][:, 0:128], w=["wbh"], r=[bank],
                             grp=(b4 > 0))
                        pd, pdn = tb(6)
                        p.op("vector", "tensor_sub", pd[:, 0:128],
                             pb[b4 % 2][:, 0:128], wbh[:, csl],
                             w=[pdn], r=[bank, "wbh"])
                        p.op("vector", "tensor_copy", out=wbl[:, csl],
                             in_=pd[:, 0:128], w=["wbl"], r=[pdn],
                             grp=(b4 > 0))
                    gsl = slice(e * 4 * 128, (e + 1) * 4 * 128)
                    p.dma("gpsimd", wuTh[dc, :, gsl], wbh[:, 0:512],
                          w=["WUT"], r=["wbh"])
                    p.dma("gpsimd", wuTl[dc, :, gsl], wbl[:, 0:512],
                          w=["WUT"], r=["wbl"])
            for s in range(NS):
                p.dma("sync", af1[:, 0:D], L1WSD[s * 128:(s + 1) * 128, :],
                      w=["wsl"])
                tstash(D, wsdTh[s, :, :], wsdTl[s, :, :], "WSDT")
            for dc in range(NCH):
                p.dma("sync", af1[:, 0:512],
                      L1WSU[dc * 128:(dc + 1) * 128, :], w=["wsl"])
                tstash(512, wsuTh[dc, :, :], wsuTl[dc, :, :], "WSUT")

        def moe_gate_prep():
            p.fence()
            p.dma("sync", af1[0:8, 0:D], L1GW[:, :], w=["wsl"])
            for c in range(NCH):
                bank = PB[c % 2]
                p.op("tensor", "transpose", pb[c % 2][0:128, 0:8],
                     af1[0:8, c * 128:(c + 1) * 128], ident[0:8, 0:8],
                     w=[bank], r=["wsl", "ident"])
                csl = slice(c * 8, (c + 1) * 8)
                p.op("scalar", "copy", out=gTh_sb[:, csl],
                     in_=pb[c % 2][0:128, 0:8], w=["gT"], r=[bank])
                pd, pdn = tb(6)
                p.op("vector", "tensor_sub", pd[:, 0:8],
                     pb[c % 2][0:128, 0:8], gTh_sb[:, csl],
                     w=[pdn], r=[bank, "gT"])
                p.op("vector", "tensor_copy", out=gTl_sb[:, csl],
                     in_=pd[:, 0:8], w=["gT2"], r=[pdn])

        def moe(sub):
            p.fence()
            src = xsrc(sub)
            dst = xdst(sub)
            keys = xkeys(sub)
            hoff, goff, woff = 0, NEB * QB, NEB * QB + NS * QB
            for t in range(NB):
                ln_tile(t, src(t), keys(t),
                        lambda c: (xlnh[:, c * QB:(c + 1) * QB], "XH"),
                        lambda c: (xlnl[:, c * QB:(c + 1) * QB], "XL"))
                for c in range(NCH):
                    cs = slice(c * QB, (c + 1) * QB)
                    gsl = slice(c * 8, (c + 1) * 8)
                    last = (c == NCH - 1)
                    p.op("tensor", "matmul", pb[0][0:8, :], gTh_sb[:, gsl],
                         xlnh[:, cs], start=(c == 0), stop=False, w=[PB[0]],
                         r=["gT", "XH"], grp=(c > 0))
                    p.op("tensor", "matmul", pb[0][0:8, :], gTl_sb[:, gsl],
                         xlnh[:, cs], start=False, stop=False, w=[PB[0]],
                         r=["gT2", "XH"], grp=True)
                    p.op("tensor", "matmul", pb[0][0:8, :], gTh_sb[:, gsl],
                         xlnl[:, cs], start=False, stop=last, w=[PB[0]],
                         r=["XL", "gT"], grp=True)
                lg, lgn = tb(0)
                p.op("vector", "tensor_copy", out=lg[0:8, :],
                     in_=pb[0][0:8, :], w=[lgn], r=[PB[0]])
                lt, ltn = tb(1)
                for st in range(4):
                    bank = PB[1 + st % 2]
                    p.op("tensor", "transpose", pb[1 + st % 2][0:128, 0:8],
                         lg[0:8, st * 128:(st + 1) * 128], ident[0:8, 0:8],
                         w=[bank], r=[lgn, "ident"])
                    p.op("vector", "tensor_copy",
                         out=lt[:, st * 8:(st + 1) * 8],
                         in_=pb[1 + st % 2][0:128, 0:8], w=[ltn], r=[bank])
                cmb, cmbn = tb(2)
                sc, scn = tb(3)
                for st in range(4):
                    lsl = slice(st * 8, (st + 1) * 8)
                    m1c, m2c = sc[:, 0:1], sc[:, 1:2]
                    w1c, w2c = sc[:, 2:3], sc[:, 3:4]
                    dfc = sc[:, 4:5]
                    msk1 = sc[:, 8:16]
                    l2b = sc[:, 16:24]
                    msk2 = sc[:, 24:32]
                    p.op("vector", "reduce_max", m1c, lt[:, lsl], axis=AX.X,
                         w=[scn], r=[ltn])
                    p.op("vector", "tensor_scalar", out=msk1, in0=lt[:, lsl],
                         scalar1=m1c, scalar2=None, op0=OP.is_equal,
                         w=[scn], r=[ltn, scn])
                    p.op("vector", "scalar_tensor_tensor", out=l2b, in0=msk1,
                         scalar=MASKNEG, in1=lt[:, lsl], op0=OP.mult,
                         op1=OP.add, w=[scn], r=[scn, ltn])
                    p.op("vector", "reduce_max", m2c, l2b, axis=AX.X,
                         w=[scn], r=[scn])
                    p.op("vector", "tensor_sub", dfc, m1c, m2c, w=[scn],
                         r=[scn])
                    p.op("scalar", "activation", out=w1c, in_=dfc,
                         func=AF.Sigmoid, bias=0.0, scale=1.0, w=[scn],
                         r=[scn])
                    p.op("vector", "tensor_scalar", out=w2c, in0=w1c,
                         scalar1=-1.0, scalar2=1.0, op0=OP.mult, op1=OP.add,
                         w=[scn], r=[scn])
                    p.op("vector", "tensor_scalar", out=msk2, in0=l2b,
                         scalar1=m2c, scalar2=None, op0=OP.is_equal,
                         w=[scn], r=[scn])
                    p.op("vector", "tensor_scalar", out=msk1, in0=msk1,
                         scalar1=w1c, scalar2=None, op0=OP.mult, w=[scn],
                         r=[scn])
                    p.op("vector", "tensor_scalar", out=msk2, in0=msk2,
                         scalar1=w2c, scalar2=None, op0=OP.mult, w=[scn],
                         r=[scn])
                    p.op("vector", "tensor_add", cmb[:, lsl], msk1, msk2,
                         w=[cmbn], r=[scn])
                cfm, cfmn = tb(0)
                for st in range(4):
                    bank = PB[1 + st % 2]
                    p.op("tensor", "transpose", pb[1 + st % 2][0:8, 0:128],
                         cmb[:, st * 8:(st + 1) * 8], ident[:, :],
                         w=[bank], r=[cmbn, "ident"])
                    p.op("vector", "tensor_copy",
                         out=cfm[0:8, st * 128:(st + 1) * 128],
                         in_=pb[1 + st % 2][0:8, 0:128], w=[cfmn], r=[bank])
                for s in range(NS):
                    p.dma("sync", wbh2[:, :], wsdTh[s, :, :], w=["wbh2"],
                          r=["WSDT"])
                    p.dma("sync", wbl2[:, :], wsdTl[s, :, :], w=["wbl2"],
                          r=["WSDT"])
                    bank = PB[4 + s % 2]
                    bk = pb[4 + s % 2]
                    for c in range(NCH):
                        cs = slice(c * QB, (c + 1) * QB)
                        last = (c == NCH - 1)
                        p.op("tensor", "matmul", bk[:, :],
                             wbh2[:, c * 128:(c + 1) * 128], xlnh[:, cs],
                             start=(c == 0), stop=False, w=[bank],
                             r=["wbh2", "XH"], grp=(c > 0))
                        p.op("tensor", "matmul", bk[:, :],
                             wbl2[:, c * 128:(c + 1) * 128], xlnh[:, cs],
                             start=False, stop=False, w=[bank],
                             r=["wbl2", "XH"], grp=True)
                        p.op("tensor", "matmul", bk[:, :],
                             wbh2[:, c * 128:(c + 1) * 128], xlnl[:, cs],
                             start=False, stop=last, w=[bank],
                             r=["XL", "wbh2"], grp=True)
                    ta, tan = tb(4)
                    p.op("scalar", "activation", out=ta[:, :], in_=bk[:, :],
                         func=AF.Gelu, bias=bsd_sb[:, s:s + 1], scale=1.0,
                         w=[tan], r=[bank, "bsd"])
                    gsl2 = slice(goff + s * QB, goff + (s + 1) * QB)
                    split(ta[:, :], tan, big1[:, gsl2], "HH", 5,
                          big2[:, gsl2], "HL", QB)
                bcb, bcbn = tb(3)
                for eb in range(NEB):
                    e = eb // 4
                    if eb % 4 == 0:
                        erow = rows[0:1, 5 * QB:6 * QB]
                        p.op("tensor", "matmul", pb[7][0:1, :],
                             ident[0:8, e:e + 1], cfm[0:8, 0:QB],
                             start=True, stop=True, w=[PB[7]],
                             r=[cfmn, "ident"])
                        p.op("vector", "tensor_copy", out=erow,
                             in_=pb[7][0:1, :], w=["rows"],
                             r=[PB[7], "rows"])
                        p.op("tensor", "matmul", pb[6][:, :], ones1[:, :],
                             erow, start=True, stop=True,
                             w=[PB[6]], r=["rows", "ones1"])
                        p.op("vector", "tensor_copy", out=bcb[:, :],
                             in_=pb[6][:, :], w=[bcbn], r=[PB[6]])
                    par = eb % 2
                    sh = wbh2 if par == 0 else wbh
                    shn = "wbh2" if par == 0 else "wbh"
                    sl = wbl2 if par == 0 else wbl
                    sln = "wbl2" if par == 0 else "wbl"
                    p.dma("sync", sh[:, :], wdTh[eb, :, :], w=[shn],
                          r=["WDT"])
                    p.dma("sync", sl[:, :], wdTl[eb, :, :], w=[sln],
                          r=["WDT"])
                    bank = PB[4 + eb % 2]
                    bk = pb[4 + eb % 2]
                    for c in range(NCH):
                        cs = slice(c * QB, (c + 1) * QB)
                        last = (c == NCH - 1)
                        p.op("tensor", "matmul", bk[:, :],
                             sh[:, c * 128:(c + 1) * 128], xlnh[:, cs],
                             start=(c == 0), stop=False, w=[bank],
                             r=[shn, "XH"], grp=(c > 0))
                        p.op("tensor", "matmul", bk[:, :],
                             sl[:, c * 128:(c + 1) * 128], xlnh[:, cs],
                             start=False, stop=False, w=[bank],
                             r=[sln, "XH"], grp=True)
                        p.op("tensor", "matmul", bk[:, :],
                             sh[:, c * 128:(c + 1) * 128], xlnl[:, cs],
                             start=False, stop=last, w=[bank],
                             r=["XL", shn], grp=True)
                    ta, tan = tb(4)
                    p.op("scalar", "activation", out=ta[:, :], in_=bk[:, :],
                         func=AF.Gelu, bias=bd_sb[:, eb:eb + 1], scale=1.0,
                         w=[tan], r=[bank, "bd"])
                    p.op("vector", "tensor_mul", ta[:, :], ta[:, :],
                         bcb[:, :], w=[tan], r=[tan, bcbn])
                    hsl = slice(hoff + eb * QB, hoff + (eb + 1) * QB)
                    split(ta[:, :], tan, big1[:, hsl], "HH", 5,
                          big2[:, hsl], "HL", QB)
                for dc in range(NCH):
                    if dc % 4 == 0:
                        xbg, xgn = xbb[(dc // 4) % 2], f"xbb{(dc // 4) % 2}"
                        p.dma("sync", xbg[:, :], src4(src(t), dc), w=[xgn],
                              r=[keys(t)(dc + k) for k in range(4)])
                    ws2a = slice(woff, woff + NEB * 128)
                    p.dma("sync", big1[:, ws2a], wuTh[dc, :, :], w=["WU"],
                          r=["WUT"])
                    p.dma("sync", big2[:, ws2a], wuTl[dc, :, :], w=["WU2"],
                          r=["WUT"])
                    p.dma("sync", wbh[:, 0:512], wsuTh[dc, :, :], w=["wbh"],
                          r=["WSUT"])
                    p.dma("sync", wbl[:, 0:512], wsuTl[dc, :, :], w=["wbl"],
                          r=["WSUT"])
                    bank = PB[4 + dc % 2]
                    bk = pb[4 + dc % 2]
                    for eb in range(NEB):
                        hsl = slice(hoff + eb * QB, hoff + (eb + 1) * QB)
                        ws2 = slice(woff + eb * 128, woff + (eb + 1) * 128)
                        p.op("tensor", "matmul", bk[:, :], big1[:, ws2],
                             big1[:, hsl], start=(eb == 0), stop=False,
                             w=[bank], r=["WU", "HH"], grp=(eb > 0))
                        p.op("tensor", "matmul", bk[:, :], big2[:, ws2],
                             big1[:, hsl], start=False, stop=False,
                             w=[bank], r=["WU2", "HH"], grp=True)
                        p.op("tensor", "matmul", bk[:, :], big1[:, ws2],
                             big2[:, hsl], start=False, stop=False,
                             w=[bank], r=["HL", "WU"], grp=True)
                    for s in range(NS):
                        gsl2 = slice(goff + s * QB, goff + (s + 1) * QB)
                        ssl = slice(s * 128, (s + 1) * 128)
                        p.op("tensor", "matmul", bk[:, :], wbh[:, ssl],
                             big1[:, gsl2], start=False, stop=False,
                             w=[bank], r=["wbh", "HH"], grp=True)
                        p.op("tensor", "matmul", bk[:, :], wbl[:, ssl],
                             big1[:, gsl2], start=False, stop=False,
                             w=[bank], r=["wbl", "HH"], grp=True)
                        p.op("tensor", "matmul", bk[:, :], wbh[:, ssl],
                             big2[:, gsl2], start=False, stop=False,
                             w=[bank], r=["HL", "wbh", "wbl"], grp=True)
                    p.op("tensor", "matmul", bk[:, :],
                         bu_sb[:, dc * 128:(dc + 1) * 128], cfm[0:8, 0:QB],
                         start=False, stop=True, w=[bank],
                         r=["bu", cfmn, "WU", "WU2", "HH", "HL"], grp=True)
                    ta, tan = tb(4 + dc % 2)
                    p.op("vector", "tensor_scalar", out=ta[:, :],
                         in0=bk[:, :], scalar1=0.5,
                         scalar2=bsuh_sb[:, dc:dc + 1],
                         op0=OP.mult, op1=OP.add,
                         w=[tan], r=[bank, "bsuh"])
                    xb, xn = xbb[(dc // 4) % 2], f"xbb{(dc // 4) % 2}"
                    tc, tcn = tb(6)
                    p.op("vector", "tensor_add", tc[:, :], ta[:, :],
                         xb[:, (dc % 4) * QB:(dc % 4 + 1) * QB],
                         w=[tcn], r=[tan, xn])
                    p.dma("gpsimd", dst(dc, t), tc[:, :],
                          w=[XK(dc, t)], r=[tcn])

        l0_quant()
        l0_ffn(0)
        if PHASES >= 2:
            attention(1, 0)
        if PHASES >= 3:
            l0_ffn(2)
        if PHASES >= 4:
            moe_prep()
            moe_gate_prep()
            moe(3)
        if PHASES >= 5:
            attention(4, 1)
        if PHASES >= 6:
            moe_gate_prep()
            moe(5)

        fin = p.barrier()
        p.q["sync"].append((tuple(fin), lambda eng: eng.nop(), None, 0))

        with nc.Block() as block:
            mk = p.emitter(sems)
            block.sync(mk("sync"))
            block.gpsimd(mk("gpsimd"))
            block.scalar(mk("scalar"))
            block.vector(mk("vector"))
            block.tensor(mk("tensor"))

    return nc



def _core_slots(cls):
    return A_SLOTS if cls == 0 else B_SLOTS


def _make_masks(cls):
    import ml_dtypes
    slots = _core_slots(cls)
    m = np.zeros((NB, 8, 128, QB), np.float32)
    qr = np.arange(QB)
    kr = np.arange(128)
    for j in range(NB):
        E = EXT[j]
        g = slots[j]
        qpos = g * QB + qr
        for k in range(8):
            c = E - 8 + k
            pos = c // 4
            jj = pos // 2
            sbk = pos if jj < 2 else pos ^ 1
            kpos = sbk * QB + (c % 4) * 128 + kr
            bad = kpos[:, None] > qpos[None, :]
            m[j, k][bad] = MASKNEG
    return m.astype(ml_dtypes.bfloat16)


_NC_CACHE = {}


def kernel(**inputs):
    x = np.asarray(inputs["L0_attn_x"], dtype=np.float32)
    names = [
        "L0_attn_W_K", "L0_attn_W_Q", "L0_attn_W_V", "L0_attn_W_O",
        "L0_ffn_gate_w", "L0_ffn_up_w", "L0_ffn_down_w",
        "L0_ffn_gate_scale", "L0_ffn_up_scale", "L0_ffn_down_scale",
        "L1_attn_Wqkv", "L1_attn_bqkv", "L1_attn_Wout", "L1_attn_bout",
        "L1_ffn_gate_w", "L1_ffn_wd", "L1_ffn_bd", "L1_ffn_wu", "L1_ffn_bu",
        "L1_ffn_ws_d", "L1_ffn_bs_d", "L1_ffn_ws_u", "L1_ffn_bs_u",
    ]
    wmap = {n: np.ascontiguousarray(np.asarray(inputs[n], dtype=np.float32))
            for n in names}

    if PHASES not in _NC_CACHE:
        _NC_CACHE[PHASES] = build_program()
    nc = _NC_CACHE[PHASES]

    in_maps = []
    for core in range(8):
        s, cls = core // 2, core % 2
        slots = _core_slots(cls)
        xb = x[s].reshape(8, QB, D)[slots]
        x0fm = np.ascontiguousarray(xb.reshape(TPC, D).T)
        m = {"x0_fm": x0fm, "masks": _make_masks(cls)}
        m.update(wmap)
        in_maps.append(m)

    res = run_bass_kernel_spmd(nc, in_maps, core_ids=list(range(8)))

    out = np.empty((4, 4096, D), np.float32)
    for core in range(8):
        s, cls = core // 2, core % 2
        slots = _core_slots(cls)
        yfm = np.asarray(res.results[core]["y_fm"])
        yt = yfm.T.reshape(NB, QB, D)
        for j, g in enumerate(slots):
            out[s, g * QB:(g + 1) * QB, :] = yt[j]
    return out
